# version 14
# baseline (speedup 1.0000x reference)
import numpy as np
import ml_dtypes
from contextlib import ExitStack

import concourse.bass as bass
import concourse.mybir as mybir
from concourse.bass_utils import run_bass_kernel_spmd
from concourse.aot_env import allow_ici_all_to_all

F32 = mybir.dt.float32
BF16 = mybir.dt.bfloat16
AF = mybir.ActivationFunctionType
ALU = mybir.AluOpType
BF = ml_dtypes.bfloat16

B, S, D = 2, 8192, 2048
H, HD = 16, 128
INTER = 8192
KVC = 4096
T = 2048
NT = T // 128
ND = D // 128
NI = INTER // 128
NQT_B = S // 512
ISC = float(1.0 / np.sqrt(HD))
SQ2 = float(np.sqrt(2.0))
EPS = 1e-5
G4 = [[0, 1, 2, 3], [4, 5, 6, 7]]
G8 = [[0, 1, 2, 3, 4, 5, 6, 7]]
FFN_SPLITS = [(512 * i, 512) for i in range(4)]


class NullEngine:
    def __getattr__(self, name):
        return lambda *a, **k: None


_NULL = NullEngine()


class Em:
    def __init__(self, me, eng, sems):
        self.me = me
        self._eng = eng
        self.sems = sems
        self.cnt = {k: 0 for k in sems}

    def g(self, name):
        return self._eng if self.me == name else _NULL

    def inc(self, ins, sem, n=1):
        self.cnt[sem] += n
        if ins is not None:
            ins.then_inc(self.sems[sem], n)
        return self.cnt[sem]

    def wait(self, eng_name, sem, val=None):
        if val is None:
            val = self.cnt[sem]
        if val > 0:
            self.g(eng_name).wait_ge(self.sems[sem], val)

    def edge(self, eng_name, ins):
        sem = {"tensor": "mm", "scalar": "ex", "vector": "vv",
               "sync": "st", "gpsimd": "gg"}[eng_name]
        v = self.inc(ins, sem)
        self.g(eng_name).wait_ge(self.sems[sem], v)
        return v


def build_program(taps=False):
    nc = bass.Bass("TRN2", target_bir_lowering=False, debug=False)

    def inp(name, shape, dt=F32):
        return nc.dram_tensor(name, shape, dt, kind="ExternalInput").ap()

    x_my = inp("x_my", [T, D])
    wq_my = inp("wq_my", [D, 512])
    wk_my = inp("wk_my", [D, 512])
    wv_my = inp("wv_my", [D, 512])
    bq_my = inp("bq_my", [128, 4])
    bk_my = inp("bk_my", [128, 4])
    bv_row = inp("bv_row", [1, 512])
    wp_full = inp("wp_full", [D, D])
    bp_row = inp("bp_row", [1, D])
    wup_my = inp("wup_my", [1024, D])
    wgate_my = inp("wgate_my", [1024, D])
    wdown_my = inp("wdown_my", [256, INTER])
    su_ug_t = inp("su_ug_t", [128, ND])
    su_dp_t = inp("su_dp_t", [128, NI])
    wsc_up = inp("wsc_up", [1, 1024])
    wsc_gate = inp("wsc_gate", [1, 1024])
    wsc_dp = inp("wsc_dp", [1, 256])
    w_kv = inp("w_kv", [D, KVC])
    bkv_row = inp("bkv_row", [1, KVC])
    cos_my = inp("cos_my", [T, 1024])
    sin_my = inp("sin_my", [T, 1024])

    y_my = nc.dram_tensor("y_my", [T, D], F32, kind="ExternalOutput").ap()
    tap = {}
    if taps:
        for nm, shp, dt in [("t_x1", [NT * 128, D], F32),
                            ("t_x2", [NT * 128, D], F32),
                            ("t_x3", [NT * 128, D], F32),
                            ("t_xn3", [NT * 128, D], F32),
                            ("t_agxn", [4 * D, T], BF16),
                            ("t_fup", [8 * D, 1024], BF16),
                            ("t_fdn", [8 * INTER, 256], BF16),
                            ("t_ctx0", [1024, S], BF16),
                            ("t_ctx1", [1024, S], BF16),
                            ("t_fin", [D, 1024], BF16),
                            ("t_ar", [1, D], F32)]:
            tap[nm] = nc.dram_tensor(nm, shp, dt, kind="ExternalOutput").ap()

    def sylvester(n):
        h = np.array([[1.0]])
        while h.shape[0] < n:
            h = np.block([[h, h], [h, -h]])
        return h

    h128_d = nc.inline_tensor(sylvester(128).astype(BF), "h128_c").ap()
    ident_d = nc.inline_tensor(np.eye(128, dtype=BF), "ident_c").ap()
    jj = np.arange(896)[None, :]
    kk = np.arange(128)[:, None]
    mask_d = nc.inline_tensor((jj >= kk + 384).astype(BF), "mask_c").ap()
    onescol_bf_d = nc.inline_tensor(np.ones((128, 1), dtype=BF), "oc_bf").ap()
    onescol_f_d = nc.inline_tensor(np.ones((128, 1), dtype=np.float32), "oc_f").ap()
    onesrow_bf_d = nc.inline_tensor(np.ones((1, 128), dtype=BF), "or_bf").ap()
    onesrow_f_d = nc.inline_tensor(np.ones((1, 128), dtype=np.float32), "or_f").ap()
    scon_d = nc.inline_tensor(
        np.tile(np.array([[np.pi / 2, -SQ2]], dtype=np.float32), (128, 1)),
        "scon_c").ap()

    def scr(name, shape, dt=F32):
        return nc.dram_tensor(name, shape, dt, kind="Internal").ap()

    x1_scr = scr("x1_scr", [NT, 128, D])
    x2_scr = scr("x2_scr", [NT, 128, D])
    x3_scr = scr("x3_scr", [NT, 128, D])
    xn3_scr = scr("xn3_scr", [NT, 128, D])
    agxn_in = scr("agxn_in", [D, T], BF16)
    agxn_out = scr("agxn_out", [4 * D, T], BF16)
    fup_in = scr("fup_in", [D, 1024], BF16)
    fup_out = scr("fup_out", [8 * D, 1024], BF16)
    fga_in = scr("fga_in", [D, 1024], BF16)
    fga_out = scr("fga_out", [8 * D, 1024], BF16)
    fdn_in = scr("fdn_in", [INTER, 256], BF16)
    fdn_out = scr("fdn_out", [8 * INTER, 256], BF16)
    ctxg_in = [scr(f"ctxg_in{g}", [256, S], BF16) for g in range(2)]
    ctxg_out = [scr(f"ctxg_out{g}", [1024, S], BF16) for g in range(2)]
    ar_in = scr("ar_in", [1, D])
    ar_out = scr("ar_out", [1, D])

    ctx = ExitStack()
    with ctx, allow_ici_all_to_all():
        pool = ctx.enter_context(nc.sbuf_tensor("pool", [128, 106368], BF16))

        def vb(off, n):
            assert off % 2 == 0
            return pool[:, off // 2: off // 2 + n]

        def vf(off, n):
            assert off % 4 == 0
            return pool[:, off // 2: off // 2 + 2 * n].bitcast(F32)

        cur = [0]

        def alloc(nbytes, align=64):
            cur[0] = (cur[0] + align - 1) // align * align
            off = cur[0]
            cur[0] += nbytes
            return off

        h128_sb = vb(alloc(256), 128)
        ident_sb = vb(alloc(256), 128)
        mask_sb = vb(alloc(1792), 896)
        oc_bf = vb(alloc(64), 1)
        oc_f = vf(alloc(64), 1)
        or_bf = vb(alloc(256), 128)[0:1, :]
        or_f = vf(alloc(512), 128)[0:1, :]
        scon = vf(alloc(64), 2)
        su_ug_sb = vf(alloc(64), ND)
        su_dp_sb = vf(alloc(256), NI)
        bq_sb = vf(alloc(64), 4)
        bk_sb = vf(alloc(64), 4)
        bv_sb = vb(alloc(1024), 512)[0:1, :]
        st6 = [vf(alloc(128), 24), vf(alloc(128), 24)]
        mv = [vf(alloc(64), 2), vf(alloc(64), 2)]
        veps = [vf(alloc(64), 1), vf(alloc(64), 1)]
        rstd = [vf(alloc(64), 1), vf(alloc(64), 1)]
        nb = [vf(alloc(64), 1), vf(alloc(64), 1)]
        row_a = vf(alloc(2048), 512)[0:1, :]
        row_b = vf(alloc(2048), 512)[0:1, :]
        g_f = [vf(alloc(2048), 512), vf(alloc(2048), 512)]

        kv2 = vf(alloc(4096), 1024)
        kvK = kv2[:, 0:512]
        kvV = kv2[:, 512:1024]
        kvt = vf(alloc(4096), 1024)
        kvbK = vf(alloc(2048), 512)
        kvbV = vf(alloc(2048), 512)
        lnx = [vb(alloc(4096), 2048), vb(alloc(4096), 2048)]
        xf_a = vf(alloc(8192), 2048)
        xf_c = vf(alloc(8192), 2048)
        xf_e = vf(alloc(8192), 2048)
        PB = alloc(0, align=512)
        assert PB + 149504 <= 212736, PB

        def pb_bf(off, n):
            return vb(PB + off, n)

        def pb_f(off, n):
            return vf(PB + off, n)

        psG = ctx.enter_context(nc.psum_tensor("psG", [128, 1024], F32))
        psH = ctx.enter_context(nc.psum_tensor("psH", [128, 1024], F32))
        psC = ctx.enter_context(nc.psum_tensor("psC", [128, 512], F32))
        psE = ctx.enter_context(nc.psum_tensor("psE", [128, 512], F32))
        psL = ctx.enter_context(nc.psum_tensor("psL", [1, 512], F32))
        psT = ctx.enter_context(nc.psum_tensor("psT", [128, 512], BF16))
        psA = psG[:, 0:512]
        psB = psH[:, 0:512]
        psD = psG[:, 512:1024]
        psF = psH[:, 512:1024]

        sems = {k: ctx.enter_context(nc.semaphore(k)) for k in
                ["ldw", "ldg", "ld1", "ld2", "ld3", "ldr", "lds", "ldq",
                 "st", "sta", "stb", "gg", "mm", "ex", "vv", "cc"]}

        block = ctx.enter_context(nc.Block())

        def emit(me, eng):
            E = Em(me, eng, sems)
            sy, te, sc, ve, gp = (E.g("sync"), E.g("tensor"), E.g("scalar"),
                                  E.g("vector"), E.g("gpsimd"))

            def ln_tile(x_f32_ap, out_bf_ap, pre_apply_wait=None, bi=0):
                st6_, mv_, veps_, rstd_, nb_ = (st6[bi], mv[bi], veps[bi],
                                                rstd[bi], nb[bi])
                last = None
                for c2 in range(4):
                    last = ve.bn_stats(st6_[:, c2 * 6:(c2 + 1) * 6],
                                       x_f32_ap[:, c2 * 512:(c2 + 1) * 512])
                E.edge("vector", last)
                E.edge("vector", ve.bn_aggr(mv_[:, :], st6_[:, :]))
                v = E.inc(ve.tensor_scalar(veps_[:, :], mv_[:, 1:2], EPS, None,
                                           op0=ALU.add), "vv")
                E.wait("scalar", "vv", v)
                E.edge("scalar", sc.activation(rstd_[:, :], veps_[:, :],
                                               AF.Ln))
                v = E.inc(sc.activation(rstd_[:, :], rstd_[:, :], AF.Exp,
                                        scale=-0.5), "ex")
                E.wait("vector", "ex", v)
                E.edge("vector", ve.tensor_mul(nb_[:, :], rstd_[:, :],
                                               rstd_[:, :]))
                E.edge("vector", ve.tensor_mul(nb_[:, :], nb_[:, :],
                                               veps_[:, :]))
                E.edge("vector", ve.tensor_scalar(nb_[:, :], nb_[:, :], -0.5,
                                                  1.5, op0=ALU.mult,
                                                  op1=ALU.add))
                E.edge("vector", ve.tensor_mul(rstd_[:, :], rstd_[:, :],
                                               nb_[:, :]))
                E.edge("vector", ve.scalar_tensor_tensor(
                    nb_[:, :], mv_[:, 0:1], -1.0, rstd_[:, :],
                    op0=ALU.mult, op1=ALU.mult))
                if pre_apply_wait is not None:
                    E.wait("vector", "mm", pre_apply_wait)
                return E.inc(ve.tensor_scalar(out_bf_ap, x_f32_ap,
                                              rstd_[:, 0:1], nb_[:, 0:1],
                                              op0=ALU.mult, op1=ALU.add),
                             "vv")

            def transpose16(src_bf, dst_col_of_t, dst_stride, src_vv, dst_buf):
                lastvv = 0
                lastmm = 0
                for g4i in range(4):
                    E.wait("tensor", "vv", src_vv if g4i == 0 else lastvv)
                    last = None
                    for t4 in range(4):
                        t = g4i * 4 + t4
                        last = te.transpose(psT[:, t4 * 128:(t4 + 1) * 128],
                                            src_bf[:, t * 128:(t + 1) * 128],
                                            ident_sb[:, :])
                    lastmm = E.inc(last, "mm")
                    E.wait("vector", "mm", lastmm)
                    t0 = g4i * 4
                    dst = bass.AP(dst_buf.tensor, dst_buf.offset + dst_col_of_t
                                  + t0 * dst_stride,
                                  [[dst_buf.ap[0][0], 128], [dst_stride, 4],
                                   [1, 128]])
                    lastvv = E.inc(
                        ve.tensor_copy(dst, psT[:, :].rearrange(
                            "p (t c) -> p t c", t=4)), "vv")
                    E.wait("tensor", "vv", lastvv)
                return lastmm, lastvv

            def bcast_row(row_ap_f32, dst, n512, ps0, ps1, extra=None):
                for s5 in range(n512):
                    ps = ps0 if s5 % 2 == 0 else ps1
                    v = E.inc(te.matmul(ps[:, :], or_f,
                                        row_ap_f32[:, s5 * 512:(s5 + 1) * 512],
                                        start=True, stop=True), "mm")
                    E.wait("vector", "mm", v)
                    if extra is None:
                        ins = ve.tensor_copy(dst[:, s5 * 512:(s5 + 1) * 512],
                                             ps[:, :])
                    else:
                        ins = ve.tensor_scalar(dst[:, s5 * 512:(s5 + 1) * 512],
                                               ps[:, :], extra[0], extra[1],
                                               op0=ALU.mult, op1=ALU.add)
                    vv = E.inc(ins, "vv")
                    E.wait("tensor", "vv", vv)
                return E.cnt["vv"]

            for dst, src in [
                (h128_sb, h128_d), (ident_sb, ident_d), (mask_sb, mask_d),
                (oc_bf, onescol_bf_d), (oc_f, onescol_f_d),
                (or_bf, onesrow_bf_d), (or_f, onesrow_f_d), (scon, scon_d),
                (su_ug_sb, su_ug_t), (su_dp_sb, su_dp_t),
                (bq_sb, bq_my), (bk_sb, bk_my),
            ]:
                E.inc(sy.dma_start(out=dst, in_=src), "ldw", 16)
            E.inc(gp.dma_start(out=bv_sb, in_=bv_row), "ldg", 16)
            c0_ldw = E.cnt["ldw"]
            c0_ldg = E.cnt["ldg"]

            w_sl = pb_bf(0, 16384)
            P_f = pb_f(32768, 16384)
            wT_sl = pb_bf(98304, 512)
            outst = [pb_bf(99840, 1024), pb_bf(101888, 1024)]
            tmp_f = pb_f(103936, 1024)
            wbc_f = pb_f(108032, 1024)

            def fold_matrix(w_dram, n_rt, fold_in, n_hi, wsc_row_dram,
                            su_ap, inv_sqrt):
                nr = n_rt * 128
                E.wait("gpsimd", "mm")
                v = E.inc(gp.dma_start(
                    out=w_sl[:, 0:n_rt * n_hi * 128].rearrange(
                        "p (r h) -> p r h", r=n_rt),
                    in_=w_dram.rearrange("(r p) c -> p r c", p=128)), "ldg", 16)
                E.wait("tensor", "ldg", v)
                E.wait("tensor", "ldw", c0_ldw)
                for s5 in range((nr + 511) // 512):
                    w5 = min(512, nr - s5 * 512)
                    lv = E.inc(sy.dma_start(
                        out=row_a[:, 0:w5],
                        in_=wsc_row_dram[:, s5 * 512:s5 * 512 + w5]),
                        "ldq", 16)
                    E.wait("tensor", "ldq", lv)
                    v = E.inc(te.matmul(psB[:, 0:w5], or_f, row_a[:, 0:w5],
                                        start=True, stop=True), "mm")
                    E.wait("vector", "mm", v)
                    vv = E.inc(ve.tensor_copy(
                        wbc_f[:, s5 * 512:s5 * 512 + w5], psB[:, 0:w5]), "vv")
                    E.wait("tensor", "vv", vv)
                    E.wait("sync", "vv", vv)
                n_rg = (n_rt + 3) // 4
                for hb in range(n_hi):
                    for rg in range(n_rg):
                        k4n = min(4, n_rt - rg * 4)
                        last = None
                        for k4 in range(k4n):
                            ri = rg * 4 + k4
                            src = w_sl[:, ri * (n_hi * 128) + hb * 128:
                                       ri * (n_hi * 128) + (hb + 1) * 128]
                            last = te.transpose(
                                psT[:, k4 * 128:(k4 + 1) * 128], src,
                                ident_sb[:, :])
                        v = E.inc(last, "mm")
                        E.wait("vector", "mm", v)
                        vv = E.inc(ve.tensor_copy(wT_sl[:, 0:k4n * 128],
                                                  psT[:, 0:k4n * 128]), "vv")
                        E.wait("tensor", "vv", vv)
                        last = None
                        for k4 in range(k4n):
                            last = te.matmul(psA[:, k4 * 128:(k4 + 1) * 128],
                                             h128_sb[:, :],
                                             wT_sl[:, k4 * 128:(k4 + 1) * 128],
                                             start=True, stop=True)
                        v = E.inc(last, "mm")
                        E.wait("vector", "mm", v)
                        vv = E.inc(ve.tensor_copy(
                            P_f[:, hb * nr + rg * 512:
                                hb * nr + rg * 512 + k4n * 128],
                            psA[:, 0:k4n * 128]), "vv")
                        E.wait("tensor", "vv", vv)
                nstage = int(np.log2(n_hi))
                for sti in range(nstage):
                    stride = 1 << sti
                    for pos in range(n_hi):
                        if (pos // stride) % 2 == 1:
                            continue
                        a0, a1 = pos, pos + stride
                        pa = P_f[:, a0 * nr:(a0 + 1) * nr]
                        pb2 = P_f[:, a1 * nr:(a1 + 1) * nr]
                        E.edge("vector", ve.tensor_copy(tmp_f[:, 0:nr], pa))
                        E.edge("vector", ve.tensor_add(pa, pa, pb2))
                        E.edge("vector", ve.tensor_sub(pb2, tmp_f[:, 0:nr], pb2))
                for hb in range(n_hi):
                    ob = outst[hb % 2]
                    if hb >= 2:
                        E.wait("vector", "st", st_of[hb - 2])
                    E.edge("vector", ve.tensor_mul(
                        tmp_f[:, 0:nr], P_f[:, hb * nr:(hb + 1) * nr],
                        wbc_f[:, 0:nr]))
                    vv = E.inc(ve.tensor_scalar(
                        ob[:, 0:nr], tmp_f[:, 0:nr], su_ap[:, hb:hb + 1],
                        inv_sqrt, op0=ALU.mult, op1=ALU.mult), "vv")
                    E.wait("sync", "vv", vv)
                    st_of[hb] = E.inc(sy.dma_start(
                        out=fold_in[hb * 128:(hb + 1) * 128, :],
                        in_=ob[:, 0:nr]), "st", 16)

            st_of = {}
            fold_matrix(wup_my, 8, fup_in, ND, wsc_up, su_ug_sb,
                        float(1.0 / np.sqrt(D)))
            st_of = {}
            fold_matrix(wgate_my, 8, fga_in, ND, wsc_gate, su_ug_sb,
                        float(1.0 / np.sqrt(D)))
            st_of = {}
            fold_matrix(wdown_my, 2, fdn_in, NI, wsc_dp, su_dp_sb,
                        float(1.0 / np.sqrt(INTER)))
            E.wait("gpsimd", "st")
            for fin, fout in [(fup_in, fup_out), (fga_in, fga_out),
                              (fdn_in, fdn_out)]:
                E.inc(gp.collective_compute(
                    "AllGather", ALU.bypass, replica_groups=G8,
                    ins=[fin.opt()], outs=[fout.opt()]), "cc")
            cc_fold = E.cnt["cc"]

            xnT_my = pb_bf(0, 32768)
            apply_guard = {}
            lnvv = {}
            lnvv2 = {}
            for tt in range(NT):
                xft = xf_a if tt % 2 == 0 else xf_e
                xsem = "ldr" if tt % 2 == 0 else "lds"
                if tt >= 2:
                    E.wait("sync", "vv", lnvv2.get(tt - 2, 0))
                lv = E.inc(sy.dma_start(
                    out=xft[:, :], in_=x_my[tt * 128:(tt + 1) * 128, :]),
                    xsem, 16)
                E.wait("vector", xsem, lv)
                xnb = lnx[tt % 2]
                vready = ln_tile(xft[:, :], xnb[:, :],
                                 pre_apply_wait=apply_guard.get(tt - 2),
                                 bi=tt % 2)
                lnvv2[tt] = vready
                lastmm, lastvv = transpose16(xnb, tt * 128, 2048, vready, xnT_my)
                apply_guard[tt] = lastmm
                lnvv[tt] = lastvv
            E.wait("sync", "vv")
            E.inc(sy.dma_start(
                out=agxn_in.rearrange("(t p) c -> p t c", p=128),
                in_=xnT_my[:, :].rearrange("p (t c) -> p t c", t=ND)),
                "st", 16)
            E.wait("gpsimd", "st")
            cc_xn = E.inc(gp.collective_compute(
                "AllGather", ALU.bypass, replica_groups=G4,
                ins=[agxn_in.opt()], outs=[agxn_out.opt()]), "cc")

            qT_b = pb_bf(0, 16384)
            kT_b = pb_bf(32768, 16384)
            v_b = pb_bf(65536, 16384)
            w3_b = pb_bf(98304, 12288)
            pT = [pb_bf(139264, 1024), pb_bf(141312, 1024)]
            l_inv = pb_f(143360, 512)[0:1, :]
            l_bc = pb_f(145408, 512)
            g_s = [pb_bf(147456, 512), pb_bf(148480, 512)]
            xblk = [pb_bf(122880, 8192), pb_bf(122880, 8192)]
            xblk_mm = [0]

            E.wait("gpsimd", "cc", cc_xn)
            E.wait("sync", "cc", cc_xn)
            sps_ex = [0, 0]
            pt_mm = [0, 0]
            pt_vv = [0, 0]
            st_gs = [0, 0]
            st_gs_sem = ["sta", "sta"]
            norm_vv = [0]
            for g in range(2):
                E.wait("gpsimd", "mm")
                ldg_w = 0
                for wi, wsrc in enumerate([wq_my, wk_my, wv_my]):
                    ldg_w = E.inc(gp.dma_start(
                        out=w3_b[:, wi * 4096: wi * 4096 + ND * 256].rearrange(
                            "p (t c) -> p t c", t=ND),
                        in_=wsrc[:, g * 256:(g + 1) * 256].rearrange(
                            "(t p) c -> p t c", p=128)), "ldg", 16)
                qkv_vv = 0
                for r in range(4):
                    for sp in range(4):
                        bi = 0
                        E.wait("sync", "mm", xblk_mm[0])
                        ldx = E.inc(sy.dma_start(
                            out=xblk[bi][:, :].rearrange(
                                "p (t c) -> p t c", t=ND),
                            in_=agxn_out[r * 2048:(r + 1) * 2048,
                                         sp * 512:(sp + 1) * 512].rearrange(
                                "(t p) c -> p t c", p=128)), "ldw", 16)
                        E.wait("tensor", "ldw", ldx)
                        if r == 0 and sp == 0:
                            E.wait("tensor", "ldg", ldg_w)
                            if g == 0:
                                E.wait("tensor", "ldw", c0_ldw)
                                E.wait("tensor", "ldg", c0_ldg)
                        qg = 4 * r + sp
                        for hh in range(2):
                            last = None
                            for t in range(ND):
                                last = te.matmul(
                                    (psA if hh == 0 else psB)[:, :],
                                    w3_b[:, t * 256 + hh * 128:
                                         t * 256 + hh * 128 + 128],
                                    xblk[bi][:, t * 512:(t + 1) * 512],
                                    start=(t == 0), stop=(t == ND - 1))
                            E.inc(last, "mm")
                            last = None
                            for t in range(ND):
                                last = te.matmul(
                                    (psC if hh == 0 else psD)[:, :],
                                    w3_b[:, 4096 + t * 256 + hh * 128:
                                         4096 + t * 256 + hh * 128 + 128],
                                    xblk[bi][:, t * 512:(t + 1) * 512],
                                    start=(t == 0), stop=(t == ND - 1))
                            E.inc(last, "mm")
                        mm_qk = E.cnt["mm"]
                        for sub in range(4):
                            pv = psE if sub % 2 == 0 else psF
                            last = None
                            for t in range(ND):
                                last = te.matmul(
                                    pv[:, 0:256],
                                    xblk[bi][:, t * 512 + sub * 128:
                                             t * 512 + sub * 128 + 128],
                                    w3_b[:, 2 * 4096 + t * 256:
                                         2 * 4096 + (t + 1) * 256],
                                    start=(t == 0), stop=False)
                            last = te.matmul(
                                pv[:, 0:256], or_bf,
                                bv_sb[:, g * 256:(g + 1) * 256],
                                start=False, stop=True)
                            mmv = E.inc(last, "mm")
                            xblk_mm[0] = mmv
                            E.wait("vector", "mm", mmv)
                            kt = 16 * r + 4 * sp + sub
                            vv = E.inc(ve.tensor_copy(
                                v_b[:, kt * 256:(kt + 1) * 256], pv[:, 0:256]),
                                "vv")
                            E.wait("tensor", "vv", vv)
                        E.wait("vector", "mm", mm_qk)
                        if g == 0 and r == 0 and sp == 0:
                            E.wait("vector", "st")
                        for hh in range(2):
                            E.inc(ve.tensor_scalar(
                                qT_b[:, hh * 8192 + qg * 512:
                                     hh * 8192 + (qg + 1) * 512],
                                (psA if hh == 0 else psB)[:, :],
                                bq_sb[:, g * 2 + hh:g * 2 + hh + 1], None,
                                op0=ALU.add), "vv")
                            qkv_vv = E.inc(ve.tensor_scalar(
                                kT_b[:, hh * 8192 + qg * 512:
                                     hh * 8192 + (qg + 1) * 512],
                                (psC if hh == 0 else psD)[:, :],
                                bk_sb[:, g * 2 + hh:g * 2 + hh + 1], None,
                                op0=ALU.add), "vv")
                        E.wait("tensor", "vv", qkv_vv)
                for hh in range(2):
                    for qg in range(NQT_B):
                        nk = 4 * qg + 4
                        npair = nk // 2

                        def emit_pv(p, b, vvm, npair=npair, qg=qg, hh=hh):
                            E.wait("tensor", "vv", vvm)
                            last = None
                            for h2 in range(2):
                                ki = 2 * p + h2
                                te.matmul(psE[:, :],
                                          v_b[:, ki * 256 + hh * 128:
                                              ki * 256 + hh * 128 + 128],
                                          pT[b][:, h2 * 512:(h2 + 1) * 512],
                                          start=(ki == 0),
                                          stop=(ki == nk - 1))
                                last = te.matmul(
                                    psL[:, :], oc_bf,
                                    pT[b][:, h2 * 512:(h2 + 1) * 512],
                                    start=(ki == 0), stop=(ki == nk - 1))
                            pt_mm[b] = E.inc(last, "mm")

                        pend = None
                        for p in range(npair):
                            b = p % 2
                            sps = psG if b == 0 else psH
                            E.wait("tensor", "ex", sps_ex[b])
                            if p == 0:
                                E.wait("tensor", "vv", norm_vv[0])
                            last = None
                            for h2 in range(2):
                                ki = 2 * p + h2
                                last = te.matmul(
                                    sps[:, h2 * 512:(h2 + 1) * 512],
                                    kT_b[:, hh * 8192 + ki * 128:
                                         hh * 8192 + (ki + 1) * 128],
                                    qT_b[:, hh * 8192 + qg * 512:
                                         hh * 8192 + (qg + 1) * 512],
                                    start=True, stop=True)
                            mmv = E.inc(last, "mm")
                            if pend is not None:
                                emit_pv(*pend)
                            E.wait("scalar", "mm", mmv)
                            E.wait("scalar", "vv", pt_vv[b])
                            E.wait("scalar", "mm", pt_mm[b])
                            exv = E.inc(sc.activation(pT[b][:, :], sps[:, :],
                                                      AF.Exp, scale=ISC), "ex")
                            sps_ex[b] = exv
                            E.wait("vector", "ex", exv)
                            ins = None
                            for h2 in range(2):
                                ki = 2 * p + h2
                                if ki >= 4 * qg:
                                    off = 384 - 128 * (ki - 4 * qg)
                                    ins = ve.tensor_mul(
                                        pT[b][:, h2 * 512:(h2 + 1) * 512],
                                        pT[b][:, h2 * 512:(h2 + 1) * 512],
                                        mask_sb[:, off:off + 512])
                            if ins is None:
                                ins = ve.engine_nop()
                            vvm = E.inc(ins, "vv")
                            pt_vv[b] = vvm
                            pend = (p, b, vvm)
                        emit_pv(*pend)
                        E.wait("vector", "mm")
                        E.edge("vector", ve.reciprocal(l_inv, psL[:, :]))
                        E.wait("tensor", "vv")
                        mmb = E.inc(te.matmul(psC[:, :], or_f, l_inv,
                                              start=True, stop=True), "mm")
                        E.wait("vector", "mm", mmb)
                        E.edge("vector", ve.tensor_copy(l_bc[:, :], psC[:, :]))
                        par = qg % 2
                        if st_gs[par]:
                            E.wait("vector", st_gs_sem[par], st_gs[par])
                        vvn = E.inc(ve.tensor_mul(g_s[par][:, :], psE[:, :],
                                                  l_bc[:, :]), "vv")
                        norm_vv[0] = vvn
                        E.wait("sync", "vv", vvn)
                        st_gs[par] = E.inc(sy.dma_start(
                            out=ctxg_in[g][hh * 128:(hh + 1) * 128,
                                           qg * 512:(qg + 1) * 512],
                            in_=g_s[par][:, :]), "sta" if g == 0 else "stb",
                            16)
                        st_gs_sem[par] = "sta" if g == 0 else "stb"
                E.wait("gpsimd", "sta" if g == 0 else "stb")
                E.inc(gp.collective_compute(
                    "AllGather", ALU.bypass, replica_groups=G4,
                    ins=[ctxg_in[g].opt()], outs=[ctxg_out[g].opt()]), "cc")
            cc_a2a = E.cnt["cc"]

            wp_b = pb_bf(0, 32768)
            cx_b = pb_bf(65536, 16384)
            bc_a = pb_f(98304, 2048)
            E.wait("gpsimd", "cc", cc_a2a)
            E.wait("sync", "cc", cc_a2a)
            E.wait("tensor", "cc", cc_a2a)
            E.wait("gpsimd", "mm")
            E.wait("sync", "mm")
            ldwp = 0
            for t in range(ND):
                ldwp = E.inc(gp.dma_start(
                    out=wp_b[:, t * 2048:(t + 1) * 2048],
                    in_=wp_full[t * 128:(t + 1) * 128, :]), "ldg", 16)
            lv = E.inc(sy.dma_start(out=row_a, in_=bp_row[:, 0:512]), "ldq", 16)
            lv = E.inc(sy.dma_start(out=row_b, in_=bp_row[:, 512:1024]), "ldq", 16)
            E.wait("tensor", "ldq", lv)
            bcast_row(row_a, bc_a[:, 0:512], 1, psA, psB)
            bcast_row(row_b, bc_a[:, 512:1024], 1, psA, psB)
            E.wait("sync", "vv")
            lv = E.inc(sy.dma_start(out=row_a, in_=bp_row[:, 1024:1536]), "ldq", 16)
            lv = E.inc(sy.dma_start(out=row_b, in_=bp_row[:, 1536:2048]), "ldq", 16)
            E.wait("tensor", "ldq", lv)
            bcast_row(row_a, bc_a[:, 1024:1536], 1, psA, psB)
            bcast_row(row_b, bc_a[:, 1536:2048], 1, psA, psB)
            if me == "sync":
                qq_off = (sy.partition_id() % 4) * T
            for half in range(2):
                ldc = 0
                for gH in range(2):
                    for hl in range(2):
                        if me == "sync":
                            src_ap = ctxg_out[gH].rearrange(
                                "(r p) c -> r p c", p=128)[hl::2][
                                :, :, bass.ds(qq_off + half * 1024, 1024)
                            ].rearrange("r p c -> p r c")
                        else:
                            src_ap = None
                        dst_ap = cx_b[:, :].rearrange(
                            "p (r g h c) -> p r g h c", r=4, g=2, h=2)[
                            :, :, gH, hl, :]
                        ldc = E.inc(sy.dma_start(out=dst_ap, in_=src_ap),
                                    "ld1", 16)
                E.wait("tensor", "ld1", ldc)
                if half == 0:
                    E.wait("tensor", "ldg", ldwp)
                for tt2 in range(8):
                    tt = half * 8 + tt2
                    lv = E.inc(sy.dma_start(
                        out=xf_a[:, :], in_=x_my[tt * 128:(tt + 1) * 128, :]),
                        "ldr", 16)
                    for j in range(4):
                        ps = psA if (tt2 * 4 + j) % 2 == 0 else psB
                        last = None
                        for Hh in range(16):
                            last = te.matmul(
                                ps[:, :],
                                cx_b[:, Hh * 1024 + tt2 * 128:
                                     Hh * 1024 + tt2 * 128 + 128],
                                wp_b[:, Hh * 2048 + j * 512:
                                     Hh * 2048 + (j + 1) * 512],
                                start=(Hh == 0), stop=(Hh == 15))
                        mmv = E.inc(last, "mm")
                        E.wait("vector", "mm", mmv)
                        if j == 0:
                            E.wait("vector", "ldr", lv)
                        E.edge("vector", ve.tensor_add(
                            xf_c[:, j * 512:(j + 1) * 512], ps[:, :],
                            bc_a[:, j * 512:(j + 1) * 512]))
                        vv = E.inc(ve.tensor_add(
                            xf_c[:, j * 512:(j + 1) * 512],
                            xf_c[:, j * 512:(j + 1) * 512],
                            xf_a[:, j * 512:(j + 1) * 512]), "vv")
                        E.wait("tensor", "vv", vv)
                    E.wait("sync", "vv")
                    stv = E.inc(sy.dma_start(out=x1_scr[tt], in_=xf_c[:, :]),
                                "st", 16)
                    E.wait("vector", "st", stv)
                    E.wait("sync", "vv")

            def y_ap(gi, c0, n):
                return vb(PB + gi * 1024, 512)[:, c0:c0 + n]

            def ffn(src_scr, dst_is_out, dst_scr, cc_wait):
                xnT_f = pb_bf(65536, 8192)
                slots = [pb_bf(81920 + k * 8192, 4096) for k in range(4)]
                dnA = [pb_bf(81920, 8192), pb_bf(98304, 8192)]
                dnB = pb_bf(114688, 8192)
                dres = pb_f(131072, 1024)
                E.wait("tensor", "cc", cc_wait)
                E.wait("sync", "cc", cc_wait)
                E.wait("sync", "st")
                for (tok0, ntok) in FFN_SPLITS:
                    ntt = ntok // 128
                    ag = {}
                    fln = {}
                    for tl in range(ntt):
                        tt = (tok0 // 128) + tl
                        xft = xf_a if tl % 2 == 0 else xf_e
                        xsem = "ldr" if tl % 2 == 0 else "lds"
                        if tl >= 2:
                            E.wait("sync", "vv", fln.get(tl - 2, 0))
                        lv = E.inc(sy.dma_start(out=xft[:, :],
                                                in_=src_scr[tt]), xsem, 16)
                        E.wait("vector", xsem, lv)
                        xnb = lnx[tl % 2]
                        vr = ln_tile(xft[:, :], xnb[:, :],
                                     pre_apply_wait=ag.get(tl - 2), bi=tl % 2)
                        fln[tl] = vr
                        lm, lvv = transpose16(xnb, tl * 128, 512, vr, xnT_f)
                        ag[tl] = lm
                        ln_done[0] = lvv
                    ld_slot = {}
                    mm_slot = {}
                    for ig in range(32):
                        sl_u = ig % 2
                        sl_g = 2 + ig % 2
                        i0 = ig * 2
                        r0 = i0 // 8
                        li = i0 % 8
                        if ig >= 2:
                            E.wait("sync", "mm", mm_slot.get(ig - 2, 0))
                        else:
                            E.wait("sync", "mm", dn_last[0])
                        lsem = "ld1" if ig % 2 == 0 else "ld2"
                        E.inc(sy.dma_start(
                            out=slots[sl_u][:, :].rearrange(
                                "p (t c) -> p t c", t=ND),
                            in_=fup_out[r0 * 2048:(r0 + 1) * 2048,
                                        li * 128:(li + 2) * 128].rearrange(
                                "(t p) c -> p t c", p=128)), lsem, 16)
                        ldv = E.inc(sy.dma_start(
                            out=slots[sl_g][:, :].rearrange(
                                "p (t c) -> p t c", t=ND),
                            in_=fga_out[r0 * 2048:(r0 + 1) * 2048,
                                        li * 128:(li + 2) * 128].rearrange(
                                "(t p) c -> p t c", p=128)), lsem, 16)
                        E.wait("tensor", lsem, ldv)
                        if ig == 0:
                            E.wait("tensor", "cc", cc_wait)
                        for i2 in range(2):
                            gi = i0 + i2
                            for s2 in range(1):
                                b2 = gi % 2
                                pu = psA if b2 == 0 else psB
                                pg = psC if b2 == 0 else psD
                                if ig == 0 and i2 == 0:
                                    E.wait("tensor", "vv", ln_done[0])
                                E.wait("tensor", "vv", upps.get(b2, 0))
                                E.wait("tensor", "ex", silups.get(b2, 0))
                                last = None
                                for t in range(ND):
                                    last = te.matmul(
                                        pu[:, :],
                                        slots[sl_u][:, t * 256 + i2 * 128:
                                                    t * 256 + i2 * 128 + 128],
                                        xnT_f[:, t * 512:(t + 1) * 512],
                                        start=(t == 0), stop=(t == ND - 1))
                                E.inc(last, "mm")
                                last = None
                                for t in range(ND):
                                    last = te.matmul(
                                        pg[:, :],
                                        slots[sl_g][:, t * 256 + i2 * 128:
                                                    t * 256 + i2 * 128 + 128],
                                        xnT_f[:, t * 512:(t + 1) * 512],
                                        start=(t == 0), stop=(t == ND - 1))
                                mmg = E.inc(last, "mm")
                                mm_slot[ig] = mmg
                                par = gi % 2
                                E.wait("scalar", "mm", mmg)
                                E.wait("scalar", "vv", upps.get(par, 0))
                                exv = E.inc(sc.activation(
                                    g_f[par][:, :], pg[:, :], AF.Silu), "ex")
                                silups[par] = exv
                                E.wait("vector", "ex", exv)
                                upps[par] = E.inc(ve.tensor_mul(
                                    y_ap(gi, 0, 512),
                                    g_f[par][:, :], pu[:, :]), "vv")
                    E.wait("tensor", "vv")
                    E.wait("sync", "mm")
                    for jp in range(8):
                        slA = dnA[jp % 2]
                        if jp >= 2:
                            E.wait("sync", "mm", dn_mm.get(jp - 2, 0) if True
                                   else 0)
                        dsem = "ld1" if jp % 2 == 0 else "ld2"
                        ldA = E.inc(sy.dma_start(
                            out=slA[:, :].rearrange("p (t c) -> p t c", t=32),
                            in_=fdn_out[jp * 8192: jp * 8192 + 4096,
                                        :].rearrange("(t p) c -> p t c",
                                                     p=128)), dsem, 16)
                        if jp >= 1:
                            E.wait("sync", "mm", dn_mmB.get(jp - 1, 0))
                        ldB = E.inc(sy.dma_start(
                            out=dnB[:, :].rearrange("p (t c) -> p t c", t=32),
                            in_=fdn_out[jp * 8192 + 4096: jp * 8192 + 8192,
                                        :].rearrange("(t p) c -> p t c",
                                                     p=128)), "ldq", 16)
                        oc0 = jp * 256
                        t0 = tok0 // 128
                        E.wait("sync", "vv", dnps.get(0, 0))
                        E.wait("sync", "vv", dnps.get(1, 0))
                        lv = E.inc(sy.dma_start(
                            out=dres[:, 0:ntt * 256].rearrange(
                                "p (t c) -> p t c", t=ntt),
                            in_=src_scr[t0:t0 + ntt, :,
                                        oc0:oc0 + 256].rearrange(
                                "t p c -> p t c")), "ld3", 16)
                        evv = 0
                        for tl in range(ntt):
                            tt = t0 + tl
                            pd = psE if tl % 2 == 0 else psF
                            E.wait("tensor", dsem, ldA if tl == 0 else 0)
                            E.wait("tensor", "vv", dnps.get(tl % 2, 0))
                            last = None
                            for t in range(32):
                                last = te.matmul(
                                    pd[:, 0:256],
                                    vb(PB + t * 1024, 512)[
                                        :, tl * 128: tl * 128 + 128],
                                    slA[:, t * 256:(t + 1) * 256],
                                    start=(t == 0), stop=False)
                            mmA = E.inc(last, "mm")
                            dn_mm[jp] = mmA
                            E.wait("tensor", "ldq", ldB if tl == 0 else 0)
                            last = None
                            for t in range(32, 64):
                                last = te.matmul(
                                    pd[:, 0:256],
                                    vb(PB + t * 1024, 512)[
                                        :, tl * 128: tl * 128 + 128],
                                    dnB[:, (t - 32) * 256:(t - 31) * 256],
                                    start=False, stop=(t == 63))
                            mmB = E.inc(last, "mm")
                            dn_mmB[jp] = mmB
                            dn_last[0] = mmB
                            E.wait("vector", "mm", mmB)
                            if tl == 0:
                                E.wait("vector", "ld3", lv)
                                E.wait("vector", "st", dn_st.get(jp - 1, 0))
                            evv = E.inc(ve.tensor_add(
                                xf_c[:, tl * 256:(tl + 1) * 256],
                                pd[:, 0:256],
                                dres[:, tl * 256:(tl + 1) * 256]), "vv")
                            dnps[tl % 2] = evv
                        E.wait("sync", "vv", evv)
                        if dst_is_out:
                            dst_ap = y_my[t0 * 128:(t0 + ntt) * 128,
                                          oc0:oc0 + 256].rearrange(
                                "(t p) c -> p t c", p=128)
                        else:
                            dst_ap = dst_scr[t0:t0 + ntt, :,
                                             oc0:oc0 + 256].rearrange(
                                "t p c -> p t c")
                        dn_st[jp] = E.inc(sy.dma_start(
                            out=dst_ap,
                            in_=xf_c[:, 0:ntt * 256].rearrange(
                                "p (t c) -> p t c", t=ntt)), "st", 16)

            dn_mm = {}
            dn_mmB = {}
            dnps = {}
            dn_st = {}
            upps = {}
            silups = {}
            dn_last = [0]
            ln_done = [0]
            ffn(x1_scr, False, x2_scr, cc_fold)

            xn3T = pb_bf(0, 32768)
            wKs = [pb_bf(65536, 8192), pb_bf(81920, 8192)]
            wVs = [pb_bf(98304, 8192), pb_bf(114688, 8192)]
            bc_a = pb_f(131072, 2048)
            tab = [pb_f(139264, 1024), pb_f(143360, 1024)]
            E.wait("sync", "st")
            E.wait("vector", "mm")
            E.wait("gpsimd", "mm")
            ag = {}
            xn3_st = {}
            abln = {}
            for tt in range(NT):
                if tt >= 1:
                    E.wait("sync", "vv", abln.get(tt - 1, 0))
                lv = E.inc(sy.dma_start(out=xf_a[:, :], in_=x2_scr[tt]),
                           "ldr", 16)
                E.wait("vector", "ldr", lv)
                xnb = lnx[tt % 2]
                if tt >= 2:
                    E.wait("vector", "st", xn3_st.get(tt - 2, 0))
                vr = ln_tile(xf_a[:, :], xnb[:, :],
                             pre_apply_wait=ag.get(tt - 2), bi=tt % 2)
                vr2 = E.inc(ve.tensor_scalar(
                    xf_e[:, :], xf_a[:, :], rstd[tt % 2][:, 0:1],
                    nb[tt % 2][:, 0:1], op0=ALU.mult, op1=ALU.add), "vv")
                abln[tt] = vr2
                E.wait("sync", "vv", vr2)
                xn3_st[tt] = E.inc(sy.dma_start(out=xn3_scr[tt],
                                                in_=xf_e[:, :]), "st", 16)
                E.wait("vector", "st", xn3_st[tt])
                lm, lvv = transpose16(xnb, tt * 128, 2048, vr, xn3T)
                ag[tt] = lm
            kv_mm = {}
            tab_vv = {}
            for cb in range(4):
                sl = cb % 2
                if cb >= 2:
                    E.wait("gpsimd", "mm", kv_mm.get(cb - 2, 0))
                last = E.inc(gp.dma_start(
                    out=wKs[sl][:, :].rearrange("p (t c) -> p t c", t=ND),
                    in_=w_kv[:, cb * 512:(cb + 1) * 512].rearrange(
                        "(t p) c -> p t c", p=128)), "ldg", 16)
                ldkv = E.inc(gp.dma_start(
                    out=wVs[sl][:, :].rearrange("p (t c) -> p t c", t=ND),
                    in_=w_kv[:, 2048 + cb * 512: 2048 + (cb + 1) * 512
                             ].rearrange("(t p) c -> p t c", p=128)), "ldg", 16)
                lv = E.inc(sy.dma_start(out=row_a,
                                        in_=bkv_row[:, cb * 512:(cb + 1) * 512]),
                           "ldq", 16)
                lv = E.inc(sy.dma_start(
                    out=row_b, in_=bkv_row[:, 2048 + cb * 512:
                                           2048 + (cb + 1) * 512]), "ldq", 16)
                E.wait("tensor", "ldq", lv)
                bcast_row(row_a, kvbK, 1, psC, psD)
                bcast_row(row_b, kvbV, 1, psC, psD)
                E.wait("sync", "vv")
                E.wait("tensor", "ldg", ldkv)
                for tt in range(NT):
                    last = None
                    for t in range(ND):
                        last = te.matmul(
                            psA[:, :], xn3T[:, t * 2048 + tt * 128:
                                            t * 2048 + tt * 128 + 128],
                            wKs[sl][:, t * 512:(t + 1) * 512],
                            start=(t == 0), stop=(t == ND - 1))
                    mmK = E.inc(last, "mm")
                    last = None
                    for t in range(ND):
                        last = te.matmul(
                            psB[:, :], xn3T[:, t * 2048 + tt * 128:
                                            t * 2048 + tt * 128 + 128],
                            wVs[sl][:, t * 512:(t + 1) * 512],
                            start=(t == 0), stop=(t == ND - 1))
                    mmV = E.inc(last, "mm")
                    E.wait("vector", "mm", mmV)
                    E.edge("vector", ve.tensor_add(kvK[:, :], psA[:, :],
                                                   kvbK[:, :]))
                    E.edge("vector", ve.tensor_add(kvV[:, :], psB[:, :],
                                                   kvbV[:, :]))
                    E.wait("tensor", "vv")
                    if cb < 2:
                        tb = tab[tt % 2]
                        tsem = "ld1" if tt % 2 == 0 else "ld2"
                        E.wait("sync", "vv", tab_vv.get((cb, tt - 2), 0))
                        lv = E.inc(sy.dma_start(
                            out=tb[:, 0:512],
                            in_=cos_my[tt * 128:(tt + 1) * 128,
                                       cb * 512:(cb + 1) * 512]), tsem, 16)
                        lv = E.inc(sy.dma_start(
                            out=tb[:, 512:1024],
                            in_=sin_my[tt * 128:(tt + 1) * 128,
                                       cb * 512:(cb + 1) * 512]), tsem, 16)
                        E.wait("vector", tsem, lv)
                        cK = [0]
                        cV = [0]
                        def vop(cnt, ins):
                            cnt[0] = E.inc(ins, "vv")
                        def vwait(cnt):
                            E.wait("vector", "vv", cnt[0])
                        vop(cK, ve.tensor_mul(kvt[:, 0:512], kvK, tb[:, 0:512]))
                        vop(cV, ve.tensor_mul(kvt[:, 512:1024], kvV,
                                              tb[:, 0:512]))
                        vop(cK, ve.scalar_tensor_tensor(
                            xf_c[:, 0:256], kvK[:, 1:512:2], -1.0,
                            tb[:, 512:1024:2], op0=ALU.mult, op1=ALU.mult))
                        vop(cV, ve.scalar_tensor_tensor(
                            xf_c[:, 256:512], kvV[:, 1:512:2], -1.0,
                            tb[:, 512:1024:2], op0=ALU.mult, op1=ALU.mult))
                        vwait(cK)
                        vop(cK, ve.tensor_add(kvt[:, 0:512:2],
                                              kvt[:, 0:512:2], xf_c[:, 0:256]))
                        vwait(cV)
                        vop(cV, ve.tensor_add(kvt[:, 512:1024:2],
                                              kvt[:, 512:1024:2],
                                              xf_c[:, 256:512]))
                        vop(cK, ve.scalar_tensor_tensor(
                            xf_c[:, 0:256], kvK[:, 0:512:2], 1.0,
                            tb[:, 513:1024:2], op0=ALU.mult, op1=ALU.mult))
                        vop(cV, ve.scalar_tensor_tensor(
                            xf_c[:, 256:512], kvV[:, 0:512:2], 1.0,
                            tb[:, 513:1024:2], op0=ALU.mult, op1=ALU.mult))
                        vwait(cK)
                        vop(cK, ve.tensor_add(kvt[:, 1:512:2],
                                              kvt[:, 1:512:2], xf_c[:, 0:256]))
                        vwait(cV)
                        vop(cV, ve.tensor_add(kvt[:, 513:1024:2],
                                              kvt[:, 513:1024:2],
                                              xf_c[:, 256:512]))
                        vwait(cK)
                        vwait(cV)
                        vv = E.inc(ve.tensor_copy(kv2[:, :], kvt[:, :]), "vv")
                        E.wait("scalar", "vv", vv)
                        tab_vv[(cb, tt)] = E.cnt["vv"]
                    else:
                        E.wait("scalar", "vv")
                    E.edge("scalar", sc.activation(kv2[:, :], kv2[:, :],
                                                   AF.Abs))
                    E.edge("scalar", sc.activation(
                        kv2[:, :], kv2[:, :], AF.Sin, scale=-0.125,
                        bias=scon[:, 0:1]))
                    E.edge("scalar", sc.activation(kv2[:, :], kv2[:, :],
                                                   AF.Square, scale=SQ2))
                    E.edge("scalar", sc.activation(
                        kv2[:, :], kv2[:, :], AF.Square, scale=SQ2,
                        bias=scon[:, 1:2]))
                    E.edge("scalar", sc.activation(
                        kv2[:, :], kv2[:, :], AF.Square, scale=SQ2,
                        bias=scon[:, 1:2]))
                    exv = E.inc(sc.activation(kv2[:, :], kv2[:, :],
                                              AF.Copy, bias=-1.0), "ex")
                    E.wait("vector", "ex", exv)
                    vv = E.inc(ve.tensor_mul(kvt[:, 0:512], kvK, kvV), "vv")
                    E.wait("tensor", "vv", vv)
                    kv_mm[cb] = E.inc(te.matmul(
                        psL[:, :], oc_f, kvt[:, 0:512],
                        start=(tt == 0), stop=(tt == NT - 1)), "mm")
                E.wait("vector", "mm")
                vv = E.inc(ve.tensor_copy(row_a, psL[:, :]), "vv")
                E.wait("sync", "vv", vv)
                stv = E.inc(sy.dma_start(out=ar_in[:, cb * 512:(cb + 1) * 512],
                                         in_=row_a), "st", 16)
                E.wait("tensor", "st", stv)
                E.wait("vector", "st", stv)
            E.wait("gpsimd", "st")
            cc_ar = E.inc(gp.collective_compute(
                "AllReduce", ALU.add, replica_groups=G4,
                ins=[ar_in.opt()], outs=[ar_out.opt()]), "cc")
            E.wait("sync", "cc", cc_ar)
            E.wait("tensor", "cc", cc_ar)
            for s5 in range(4):
                lv = E.inc(sy.dma_start(
                    out=row_a, in_=ar_out[:, s5 * 512:(s5 + 1) * 512]),
                    "ldq", 16)
                E.wait("tensor", "ldq", lv)
                bcast_row(row_a, bc_a[:, s5 * 512:(s5 + 1) * 512], 1, psC, psD,
                          extra=(0.5, 0.5))
                E.wait("sync", "vv")
            for tt in range(NT):
                lv = E.inc(sy.dma_start(out=xf_a[:, :], in_=x2_scr[tt]),
                           "ldw", 16)
                lv2 = E.inc(sy.dma_start(out=xf_e[:, :],
                                         in_=xn3_scr[tt]), "ldw", 16)
                E.wait("vector", "ldw", lv2)
                E.edge("vector", ve.tensor_mul(xf_c[:, :], xf_e[:, :],
                                               bc_a[:, :]))
                vv = E.inc(ve.tensor_add(xf_c[:, :], xf_c[:, :], xf_a[:, :]),
                           "vv")
                E.wait("sync", "vv", vv)
                stv = E.inc(sy.dma_start(out=x3_scr[tt], in_=xf_c[:, :]),
                            "st", 16)
                E.wait("vector", "st", stv)
                E.wait("sync", "vv", vv)

            ffn(x3_scr, True, None, cc_fold)

            if taps:
                E.wait("sync", "st")
                for nm, srcd in [("t_x1", x1_scr), ("t_x2", x2_scr),
                                 ("t_x3", x3_scr), ("t_xn3", xn3_scr)]:
                    E.inc(sy.dma_start(
                        out=tap[nm],
                        in_=srcd.rearrange("t p d -> (t p) d")), "st", 16)
                E.inc(sy.dma_start(out=tap["t_agxn"], in_=agxn_out), "st", 16)
                E.inc(sy.dma_start(out=tap["t_fup"], in_=fup_out), "st", 16)
                E.inc(sy.dma_start(out=tap["t_fdn"], in_=fdn_out), "st", 16)
                E.inc(sy.dma_start(out=tap["t_ctx0"], in_=ctxg_out[0]), "st", 16)
                E.inc(sy.dma_start(out=tap["t_ctx1"], in_=ctxg_out[1]), "st", 16)
                E.inc(sy.dma_start(out=tap["t_fin"], in_=fup_in), "st", 16)
                E.inc(sy.dma_start(out=tap["t_ar"], in_=ar_out), "st", 16)
            E.wait("sync", "st")
            E.wait("gpsimd", "st")

        @block.sync
        def _(e):
            emit("sync", e)

        @block.tensor
        def _(e):
            emit("tensor", e)

        @block.scalar
        def _(e):
            emit("scalar", e)

        @block.vector
        def _(e):
            emit("vector", e)

        @block.gpsimd
        def _(e):
            emit("gpsimd", e)

    return nc


def _prep_inputs(ins):
    x = np.asarray(ins["L0_attn_a_x"], dtype=np.float32)
    wq = np.asarray(ins["L0_attn_a_wq"], np.float32)
    bq = np.asarray(ins["L0_attn_a_bq"], np.float32)
    wk = np.asarray(ins["L0_attn_a_wk"], np.float32)
    bk = np.asarray(ins["L0_attn_a_bk"], np.float32)
    wv = np.asarray(ins["L0_attn_a_wv"], np.float32)
    bv = np.asarray(ins["L0_attn_a_bv"], np.float32)
    wp = np.ascontiguousarray(np.asarray(ins["L0_attn_a_wp"], np.float32))
    bp = np.asarray(ins["L0_attn_a_bp"], np.float32)
    wkv = np.ascontiguousarray(np.asarray(ins["L0_attn_b_W"], np.float32))
    bkv = np.asarray(ins["L0_attn_b_b"], np.float32)
    su_ug = np.asarray(ins["L0_ffn_SU_ug"], np.float32)
    su_dp = np.asarray(ins["L0_ffn_SU_dp"], np.float32)
    wsc_ug = np.asarray(ins["L0_ffn_Wscale_ug"], np.float32)
    wsc_dp = np.asarray(ins["L0_ffn_Wscale_dp"], np.float32)
    w_up = np.asarray(ins["L0_ffn_W_up"], np.float32)
    w_gate = np.asarray(ins["L0_ffn_W_gate"], np.float32)
    w_down = np.asarray(ins["L0_ffn_W_down"], np.float32)

    rot = 1024
    inv_freq = 1.0 / (10000.0 ** (np.arange(0, rot, 2, dtype=np.float64) / rot))
    su_ug_t = np.ascontiguousarray(su_ug.reshape(ND, 128).T)
    su_dp_t = np.ascontiguousarray(su_dp.reshape(NI, 128).T)

    in_maps = []
    for c in range(8):
        b = c // 4
        qq = c % 4
        h0 = 4 * qq
        pos = np.arange(qq * T, (qq + 1) * T, dtype=np.float64)
        fr = pos[:, None] * inv_freq[None, :]
        cos_e = np.repeat(np.cos(fr), 2, axis=1).astype(np.float32)
        sin_e = np.repeat(np.sin(fr), 2, axis=1).astype(np.float32)
        m = {
            "x_my": np.ascontiguousarray(x[b, qq * T:(qq + 1) * T, :]),
            "wq_my": np.ascontiguousarray(wq[:, h0 * 128:(h0 + 4) * 128]),
            "wk_my": np.ascontiguousarray(wk[:, h0 * 128:(h0 + 4) * 128]),
            "wv_my": np.ascontiguousarray(wv[:, h0 * 128:(h0 + 4) * 128]),
            "bq_my": np.ascontiguousarray(
                bq[h0 * 128:(h0 + 4) * 128].reshape(4, 128).T),
            "bk_my": np.ascontiguousarray(
                bk[h0 * 128:(h0 + 4) * 128].reshape(4, 128).T),
            "bv_row": np.ascontiguousarray(
                bv[h0 * 128:(h0 + 4) * 128].reshape(1, 512)),
            "wp_full": wp,
            "bp_row": np.ascontiguousarray(bp.reshape(1, D)),
            "wup_my": np.ascontiguousarray(w_up[c * 1024:(c + 1) * 1024, :]),
            "wgate_my": np.ascontiguousarray(
                w_gate[c * 1024:(c + 1) * 1024, :]),
            "wdown_my": np.ascontiguousarray(w_down[c * 256:(c + 1) * 256, :]),
            "su_ug_t": su_ug_t,
            "su_dp_t": su_dp_t,
            "wsc_up": np.ascontiguousarray(
                wsc_ug[c * 1024:(c + 1) * 1024].reshape(1, 1024)),
            "wsc_gate": np.ascontiguousarray(
                wsc_ug[INTER + c * 1024: INTER + (c + 1) * 1024
                       ].reshape(1, 1024)),
            "wsc_dp": np.ascontiguousarray(
                wsc_dp[c * 256:(c + 1) * 256].reshape(1, 256)),
            "w_kv": wkv,
            "bkv_row": np.ascontiguousarray(bkv.reshape(1, KVC)),
            "cos_my": cos_e,
            "sin_my": sin_e,
        }
        in_maps.append(m)
    return in_maps


def kernel(**inputs):
    nc = build_program()
    in_maps = _prep_inputs(inputs)
    res = run_bass_kernel_spmd(nc, in_maps, core_ids=list(range(8)))
    out = np.empty((B, S, D), dtype=np.float32)
    for c in range(8):
        b = c // 4
        qq = c % 4
        out[b, qq * T:(qq + 1) * T, :] = res.results[c]["y_my"]
    return out


# version 18 (final)
# speedup vs baseline: 1.0856x; 1.0856x over previous
import numpy as np
import ml_dtypes
from contextlib import ExitStack

import concourse.bass as bass
import concourse.mybir as mybir
from concourse.bass_utils import run_bass_kernel_spmd
from concourse.aot_env import allow_ici_all_to_all

F32 = mybir.dt.float32
BF16 = mybir.dt.bfloat16
AF = mybir.ActivationFunctionType
ALU = mybir.AluOpType
BF = ml_dtypes.bfloat16

B, S, D = 2, 8192, 2048
H, HD = 16, 128
INTER = 8192
KVC = 4096
T = 2048
NT = T // 128
ND = D // 128
NI = INTER // 128
NQT_B = S // 512
ISC = float(1.0 / np.sqrt(HD))
SQ2 = float(np.sqrt(2.0))
EPS = 1e-5
G4 = [[0, 1, 2, 3], [4, 5, 6, 7]]
G8 = [[0, 1, 2, 3, 4, 5, 6, 7]]
FFN_SPLITS = [(512 * i, 512) for i in range(4)]


class NullEngine:
    def __getattr__(self, name):
        return lambda *a, **k: None


_NULL = NullEngine()


class Em:
    def __init__(self, me, eng, sems):
        self.me = me
        self._eng = eng
        self.sems = sems
        self.cnt = {k: 0 for k in sems}

    def g(self, name):
        return self._eng if self.me == name else _NULL

    def inc(self, ins, sem, n=1):
        self.cnt[sem] += n
        if ins is not None:
            ins.then_inc(self.sems[sem], n)
        return self.cnt[sem]

    def wait(self, eng_name, sem, val=None):
        if val is None:
            val = self.cnt[sem]
        if val > 0:
            self.g(eng_name).wait_ge(self.sems[sem], val)

    def edge(self, eng_name, ins):
        sem = {"tensor": "mm", "scalar": "ex", "vector": "vv",
               "sync": "st", "gpsimd": "gg"}[eng_name]
        v = self.inc(ins, sem)
        self.g(eng_name).wait_ge(self.sems[sem], v)
        return v


def build_program(taps=False):
    nc = bass.Bass("TRN2", target_bir_lowering=False, debug=False)

    def inp(name, shape, dt=F32):
        return nc.dram_tensor(name, shape, dt, kind="ExternalInput").ap()

    x_my = inp("x_my", [T, D])
    wq_my = inp("wq_my", [D, 512])
    wk_my = inp("wk_my", [D, 512])
    wv_my = inp("wv_my", [D, 512])
    bq_my = inp("bq_my", [128, 4])
    bk_my = inp("bk_my", [128, 4])
    bv_row = inp("bv_row", [1, 512])
    wp_full = inp("wp_full", [D, D])
    bp_row = inp("bp_row", [1, D])
    wup_my = inp("wup_my", [1024, D])
    wgate_my = inp("wgate_my", [1024, D])
    wdown_my = inp("wdown_my", [256, INTER])
    su_ug_t = inp("su_ug_t", [128, ND])
    su_dp_t = inp("su_dp_t", [128, NI])
    wsc_up = inp("wsc_up", [1, 1024])
    wsc_gate = inp("wsc_gate", [1, 1024])
    wsc_dp = inp("wsc_dp", [1, 256])
    w_kv = inp("w_kv", [D, KVC])
    bkv_row = inp("bkv_row", [1, KVC])
    cos_my = inp("cos_my", [T, 1024])
    sin_my = inp("sin_my", [T, 1024])

    y_my = nc.dram_tensor("y_my", [T, D], F32, kind="ExternalOutput").ap()
    tap = {}
    if taps:
        for nm, shp, dt in [("t_x1", [NT * 128, D], F32),
                            ("t_x2", [NT * 128, D], F32),
                            ("t_x3", [NT * 128, D], F32),
                            ("t_xn3", [NT * 128, D], F32),
                            ("t_agxn", [4 * D, T], BF16),
                            ("t_fup", [8 * D, 1024], BF16),
                            ("t_fdn", [8 * INTER, 256], BF16),
                            ("t_ctx0", [1024, S], BF16),
                            ("t_ctx1", [1024, S], BF16),
                            ("t_fin", [D, 1024], BF16),
                            ("t_ar", [1, D], F32)]:
            tap[nm] = nc.dram_tensor(nm, shp, dt, kind="ExternalOutput").ap()

    def sylvester(n):
        h = np.array([[1.0]])
        while h.shape[0] < n:
            h = np.block([[h, h], [h, -h]])
        return h

    h128_d = nc.inline_tensor(sylvester(128).astype(BF), "h128_c").ap()
    ident_d = nc.inline_tensor(np.eye(128, dtype=BF), "ident_c").ap()
    jj = np.arange(896)[None, :]
    kk = np.arange(128)[:, None]
    mask_d = nc.inline_tensor((jj >= kk + 384).astype(BF), "mask_c").ap()
    onescol_bf_d = nc.inline_tensor(np.ones((128, 1), dtype=BF), "oc_bf").ap()
    onescol_f_d = nc.inline_tensor(np.ones((128, 1), dtype=np.float32), "oc_f").ap()
    onesrow_bf_d = nc.inline_tensor(np.ones((1, 128), dtype=BF), "or_bf").ap()
    onesrow_f_d = nc.inline_tensor(np.ones((1, 128), dtype=np.float32), "or_f").ap()
    scon_d = nc.inline_tensor(
        np.tile(np.array([[np.pi / 2, -SQ2]], dtype=np.float32), (128, 1)),
        "scon_c").ap()

    def scr(name, shape, dt=F32):
        return nc.dram_tensor(name, shape, dt, kind="Internal").ap()

    x1_scr = scr("x1_scr", [NT, 128, D])
    x2_scr = scr("x2_scr", [NT, 128, D])
    x3_scr = scr("x3_scr", [NT, 128, D])
    xn3_scr = scr("xn3_scr", [NT, 128, D])
    agxn_in = scr("agxn_in", [D, T], BF16)
    agxn_out = scr("agxn_out", [4 * D, T], BF16)
    fup_in = scr("fup_in", [D, 1024], BF16)
    fup_out = scr("fup_out", [8 * D, 1024], BF16)
    fga_in = scr("fga_in", [D, 1024], BF16)
    fga_out = scr("fga_out", [8 * D, 1024], BF16)
    fdn_in = scr("fdn_in", [INTER, 256], BF16)
    fdn_out = scr("fdn_out", [8 * INTER, 256], BF16)
    ctxg_in = [scr(f"ctxg_in{g}", [256, S], BF16) for g in range(2)]
    ctxg_out = [scr(f"ctxg_out{g}", [1024, S], BF16) for g in range(2)]
    ar_in = scr("ar_in", [1, D])
    ar_out = scr("ar_out", [1, D])

    ctx = ExitStack()
    with ctx, allow_ici_all_to_all():
        pool = ctx.enter_context(nc.sbuf_tensor("pool", [128, 106368], BF16))

        def vb(off, n):
            assert off % 2 == 0
            return pool[:, off // 2: off // 2 + n]

        def vf(off, n):
            assert off % 4 == 0
            return pool[:, off // 2: off // 2 + 2 * n].bitcast(F32)

        cur = [0]

        def alloc(nbytes, align=64):
            cur[0] = (cur[0] + align - 1) // align * align
            off = cur[0]
            cur[0] += nbytes
            return off

        h128_sb = vb(alloc(256), 128)
        ident_sb = vb(alloc(256), 128)
        mask_sb = vb(alloc(1792), 896)
        oc_bf = vb(alloc(64), 1)
        oc_f = vf(alloc(64), 1)
        or_bf = vb(alloc(256), 128)[0:1, :]
        or_f = vf(alloc(512), 128)[0:1, :]
        scon = vf(alloc(64), 2)
        su_ug_sb = vf(alloc(64), ND)
        su_dp_sb = vf(alloc(256), NI)
        bq_sb = vf(alloc(64), 4)
        bk_sb = vf(alloc(64), 4)
        bv_sb = vb(alloc(1024), 512)[0:1, :]
        st6 = [vf(alloc(128), 24), vf(alloc(128), 24)]
        mv = [vf(alloc(64), 2), vf(alloc(64), 2)]
        veps = [vf(alloc(64), 1), vf(alloc(64), 1)]
        rstd = [vf(alloc(64), 1), vf(alloc(64), 1)]
        nb = [vf(alloc(64), 1), vf(alloc(64), 1)]
        row_a = vf(alloc(2048), 512)[0:1, :]
        row_b = vf(alloc(2048), 512)[0:1, :]
        g_f = [vf(alloc(2048), 512), vf(alloc(2048), 512)]

        kv2 = vf(alloc(4096), 1024)
        kvK = kv2[:, 0:512]
        kvV = kv2[:, 512:1024]
        kvt = vf(alloc(4096), 1024)
        kvbK = vf(alloc(2048), 512)
        kvbV = vf(alloc(2048), 512)
        lnx = [vb(alloc(4096), 2048), vb(alloc(4096), 2048)]
        xf_a = vf(alloc(8192), 2048)
        xf_c = vf(alloc(8192), 2048)
        xf_e = vf(alloc(8192), 2048)
        PB = alloc(0, align=512)
        assert PB + 149504 <= 212736, PB

        def pb_bf(off, n):
            return vb(PB + off, n)

        def pb_f(off, n):
            return vf(PB + off, n)

        psG = ctx.enter_context(nc.psum_tensor("psG", [128, 1024], F32))
        psH = ctx.enter_context(nc.psum_tensor("psH", [128, 1024], F32))
        psC = ctx.enter_context(nc.psum_tensor("psC", [128, 512], F32))
        psE = ctx.enter_context(nc.psum_tensor("psE", [128, 512], F32))
        psL = ctx.enter_context(nc.psum_tensor("psL", [1, 512], F32))
        psT = ctx.enter_context(nc.psum_tensor("psT", [128, 512], BF16))
        psA = psG[:, 0:512]
        psB = psH[:, 0:512]
        psD = psG[:, 512:1024]
        psF = psH[:, 512:1024]

        sems = {k: ctx.enter_context(nc.semaphore(k)) for k in
                ["ldw", "ldg", "ld1", "ld2", "ld3", "ldr", "lds", "ldq",
                 "st", "sta", "stb", "gg", "mm", "ex", "vv", "cc"]}

        block = ctx.enter_context(nc.Block())

        def emit(me, eng):
            E = Em(me, eng, sems)
            sy, te, sc, ve, gp = (E.g("sync"), E.g("tensor"), E.g("scalar"),
                                  E.g("vector"), E.g("gpsimd"))

            def ln_tile(x_f32_ap, out_bf_ap, pre_apply_wait=None, bi=0):
                st6_, mv_, veps_, rstd_, nb_ = (st6[bi], mv[bi], veps[bi],
                                                rstd[bi], nb[bi])
                last = None
                for c2 in range(4):
                    last = ve.bn_stats(st6_[:, c2 * 6:(c2 + 1) * 6],
                                       x_f32_ap[:, c2 * 512:(c2 + 1) * 512])
                E.edge("vector", last)
                E.edge("vector", ve.bn_aggr(mv_[:, :], st6_[:, :]))
                v = E.inc(ve.tensor_scalar(veps_[:, :], mv_[:, 1:2], EPS, None,
                                           op0=ALU.add), "vv")
                E.wait("scalar", "vv", v)
                E.edge("scalar", sc.activation(rstd_[:, :], veps_[:, :],
                                               AF.Ln))
                v = E.inc(sc.activation(rstd_[:, :], rstd_[:, :], AF.Exp,
                                        scale=-0.5), "ex")
                E.wait("vector", "ex", v)
                E.edge("vector", ve.tensor_mul(nb_[:, :], rstd_[:, :],
                                               rstd_[:, :]))
                E.edge("vector", ve.tensor_mul(nb_[:, :], nb_[:, :],
                                               veps_[:, :]))
                E.edge("vector", ve.tensor_scalar(nb_[:, :], nb_[:, :], -0.5,
                                                  1.5, op0=ALU.mult,
                                                  op1=ALU.add))
                E.edge("vector", ve.tensor_mul(rstd_[:, :], rstd_[:, :],
                                               nb_[:, :]))
                E.edge("vector", ve.scalar_tensor_tensor(
                    nb_[:, :], mv_[:, 0:1], -1.0, rstd_[:, :],
                    op0=ALU.mult, op1=ALU.mult))
                if pre_apply_wait is not None:
                    E.wait("vector", "mm", pre_apply_wait)
                return E.inc(ve.tensor_scalar(out_bf_ap, x_f32_ap,
                                              rstd_[:, 0:1], nb_[:, 0:1],
                                              op0=ALU.mult, op1=ALU.add),
                             "vv")

            def transpose16(src_bf, dst_col_of_t, dst_stride, src_vv, dst_buf):
                lastvv = 0
                lastmm = 0
                for g4i in range(4):
                    E.wait("tensor", "vv", src_vv if g4i == 0 else lastvv)
                    last = None
                    for t4 in range(4):
                        t = g4i * 4 + t4
                        last = te.transpose(psT[:, t4 * 128:(t4 + 1) * 128],
                                            src_bf[:, t * 128:(t + 1) * 128],
                                            ident_sb[:, :])
                    lastmm = E.inc(last, "mm")
                    E.wait("vector", "mm", lastmm)
                    t0 = g4i * 4
                    dst = bass.AP(dst_buf.tensor, dst_buf.offset + dst_col_of_t
                                  + t0 * dst_stride,
                                  [[dst_buf.ap[0][0], 128], [dst_stride, 4],
                                   [1, 128]])
                    lastvv = E.inc(
                        ve.tensor_copy(dst, psT[:, :].rearrange(
                            "p (t c) -> p t c", t=4)), "vv")
                    E.wait("tensor", "vv", lastvv)
                return lastmm, lastvv

            def bcast_row(row_ap_f32, dst, n512, ps0, ps1, extra=None):
                for s5 in range(n512):
                    ps = ps0 if s5 % 2 == 0 else ps1
                    v = E.inc(te.matmul(ps[:, :], or_f,
                                        row_ap_f32[:, s5 * 512:(s5 + 1) * 512],
                                        start=True, stop=True), "mm")
                    E.wait("vector", "mm", v)
                    if extra is None:
                        ins = ve.tensor_copy(dst[:, s5 * 512:(s5 + 1) * 512],
                                             ps[:, :])
                    else:
                        ins = ve.tensor_scalar(dst[:, s5 * 512:(s5 + 1) * 512],
                                               ps[:, :], extra[0], extra[1],
                                               op0=ALU.mult, op1=ALU.add)
                    vv = E.inc(ins, "vv")
                    E.wait("tensor", "vv", vv)
                return E.cnt["vv"]

            for dst, src in [
                (h128_sb, h128_d), (ident_sb, ident_d), (mask_sb, mask_d),
                (oc_bf, onescol_bf_d), (oc_f, onescol_f_d),
                (or_bf, onesrow_bf_d), (or_f, onesrow_f_d), (scon, scon_d),
                (su_ug_sb, su_ug_t), (su_dp_sb, su_dp_t),
                (bq_sb, bq_my), (bk_sb, bk_my),
            ]:
                E.inc(sy.dma_start(out=dst, in_=src), "ldw", 16)
            E.inc(gp.dma_start(out=bv_sb, in_=bv_row), "ldg", 16)
            c0_ldw = E.cnt["ldw"]
            c0_ldg = E.cnt["ldg"]

            w_sl = pb_bf(0, 16384)
            P_f = pb_f(32768, 16384)
            wT_sl = pb_bf(98304, 512)
            outst = [pb_bf(99840, 1024), pb_bf(101888, 1024)]
            tmp_f = pb_f(103936, 1024)
            wbc_f = pb_f(108032, 1024)

            def fold_matrix(w_dram, n_rt, fold_in, n_hi, wsc_row_dram,
                            su_ap, inv_sqrt):
                nr = n_rt * 128
                E.wait("gpsimd", "mm")
                v = E.inc(gp.dma_start(
                    out=w_sl[:, 0:n_rt * n_hi * 128].rearrange(
                        "p (r h) -> p r h", r=n_rt),
                    in_=w_dram.rearrange("(r p) c -> p r c", p=128)), "ldg", 16)
                E.wait("tensor", "ldg", v)
                E.wait("tensor", "ldw", c0_ldw)
                for s5 in range((nr + 511) // 512):
                    w5 = min(512, nr - s5 * 512)
                    lv = E.inc(sy.dma_start(
                        out=row_a[:, 0:w5],
                        in_=wsc_row_dram[:, s5 * 512:s5 * 512 + w5]),
                        "ldq", 16)
                    E.wait("tensor", "ldq", lv)
                    v = E.inc(te.matmul(psB[:, 0:w5], or_f, row_a[:, 0:w5],
                                        start=True, stop=True), "mm")
                    E.wait("vector", "mm", v)
                    vv = E.inc(ve.tensor_copy(
                        wbc_f[:, s5 * 512:s5 * 512 + w5], psB[:, 0:w5]), "vv")
                    E.wait("tensor", "vv", vv)
                    E.wait("sync", "vv", vv)
                n_rg = (n_rt + 3) // 4
                for hb in range(n_hi):
                    for rg in range(n_rg):
                        k4n = min(4, n_rt - rg * 4)
                        last = None
                        for k4 in range(k4n):
                            ri = rg * 4 + k4
                            src = w_sl[:, ri * (n_hi * 128) + hb * 128:
                                       ri * (n_hi * 128) + (hb + 1) * 128]
                            last = te.transpose(
                                psT[:, k4 * 128:(k4 + 1) * 128], src,
                                ident_sb[:, :])
                        v = E.inc(last, "mm")
                        E.wait("vector", "mm", v)
                        vv = E.inc(ve.tensor_copy(wT_sl[:, 0:k4n * 128],
                                                  psT[:, 0:k4n * 128]), "vv")
                        E.wait("tensor", "vv", vv)
                        last = None
                        for k4 in range(k4n):
                            last = te.matmul(psA[:, k4 * 128:(k4 + 1) * 128],
                                             h128_sb[:, :],
                                             wT_sl[:, k4 * 128:(k4 + 1) * 128],
                                             start=True, stop=True)
                        v = E.inc(last, "mm")
                        E.wait("vector", "mm", v)
                        vv = E.inc(ve.tensor_copy(
                            P_f[:, hb * nr + rg * 512:
                                hb * nr + rg * 512 + k4n * 128],
                            psA[:, 0:k4n * 128]), "vv")
                        E.wait("tensor", "vv", vv)
                nstage = int(np.log2(n_hi))
                for sti in range(nstage):
                    stride = 1 << sti
                    for pos in range(n_hi):
                        if (pos // stride) % 2 == 1:
                            continue
                        a0, a1 = pos, pos + stride
                        pa = P_f[:, a0 * nr:(a0 + 1) * nr]
                        pb2 = P_f[:, a1 * nr:(a1 + 1) * nr]
                        E.edge("vector", ve.tensor_copy(tmp_f[:, 0:nr], pa))
                        E.edge("vector", ve.tensor_add(pa, pa, pb2))
                        E.edge("vector", ve.tensor_sub(pb2, tmp_f[:, 0:nr], pb2))
                for hb in range(n_hi):
                    ob = outst[hb % 2]
                    if hb >= 2:
                        E.wait("vector", "st", st_of[hb - 2])
                    E.edge("vector", ve.tensor_mul(
                        tmp_f[:, 0:nr], P_f[:, hb * nr:(hb + 1) * nr],
                        wbc_f[:, 0:nr]))
                    vv = E.inc(ve.tensor_scalar(
                        ob[:, 0:nr], tmp_f[:, 0:nr], su_ap[:, hb:hb + 1],
                        inv_sqrt, op0=ALU.mult, op1=ALU.mult), "vv")
                    E.wait("sync", "vv", vv)
                    st_of[hb] = E.inc(sy.dma_start(
                        out=fold_in[hb * 128:(hb + 1) * 128, :],
                        in_=ob[:, 0:nr]), "st", 16)

            st_of = {}
            fold_matrix(wup_my, 8, fup_in, ND, wsc_up, su_ug_sb,
                        float(1.0 / np.sqrt(D)))
            st_of = {}
            fold_matrix(wgate_my, 8, fga_in, ND, wsc_gate, su_ug_sb,
                        float(1.0 / np.sqrt(D)))
            st_of = {}
            fold_matrix(wdown_my, 2, fdn_in, NI, wsc_dp, su_dp_sb,
                        float(1.0 / np.sqrt(INTER)))
            E.wait("gpsimd", "st")
            for fin, fout in [(fup_in, fup_out), (fga_in, fga_out),
                              (fdn_in, fdn_out)]:
                E.inc(gp.collective_compute(
                    "AllGather", ALU.bypass, replica_groups=G8,
                    ins=[fin.opt()], outs=[fout.opt()]), "cc")
            cc_fold = E.cnt["cc"]

            xnT_my = pb_bf(0, 32768)
            apply_guard = {}
            lnvv = {}
            lnvv2 = {}
            for tt in range(NT):
                xft = xf_a if tt % 2 == 0 else xf_e
                xsem = "ldr" if tt % 2 == 0 else "lds"
                if tt >= 2:
                    E.wait("sync", "vv", lnvv2.get(tt - 2, 0))
                lv = E.inc(sy.dma_start(
                    out=xft[:, :], in_=x_my[tt * 128:(tt + 1) * 128, :]),
                    xsem, 16)
                E.wait("vector", xsem, lv)
                xnb = lnx[tt % 2]
                vready = ln_tile(xft[:, :], xnb[:, :],
                                 pre_apply_wait=apply_guard.get(tt - 2),
                                 bi=tt % 2)
                lnvv2[tt] = vready
                lastmm, lastvv = transpose16(xnb, tt * 128, 2048, vready, xnT_my)
                apply_guard[tt] = lastmm
                lnvv[tt] = lastvv
            E.wait("sync", "vv")
            E.inc(sy.dma_start(
                out=agxn_in.rearrange("(t p) c -> p t c", p=128),
                in_=xnT_my[:, :].rearrange("p (t c) -> p t c", t=ND)),
                "st", 16)
            E.wait("gpsimd", "st")
            cc_xn = E.inc(gp.collective_compute(
                "AllGather", ALU.bypass, replica_groups=G4,
                ins=[agxn_in.opt()], outs=[agxn_out.opt()]), "cc")

            qT_b = pb_bf(0, 16384)
            kT_b = pb_bf(32768, 16384)
            v_b = pb_bf(65536, 16384)
            w3_b = pb_bf(98304, 12288)
            pT = [pb_bf(139264, 1024), pb_bf(141312, 1024)]
            l_inv = pb_f(143360, 512)[0:1, :]
            l_bc = pb_f(145408, 512)
            g_s = [pb_bf(147456, 512), pb_bf(148480, 512)]
            xblk = [pb_bf(122880, 8192), pb_bf(122880, 8192)]
            xblk_mm = [0]

            E.wait("gpsimd", "cc", cc_xn)
            E.wait("sync", "cc", cc_xn)
            sps_ex = [0, 0]
            pt_mm = [0, 0]
            pt_vv = [0, 0]
            st_gs = [0, 0]
            st_gs_sem = ["sta", "sta"]
            norm_vv = [0]
            for g in range(2):
                E.wait("gpsimd", "mm")
                ldg_w = 0
                for wi, wsrc in enumerate([wq_my, wk_my, wv_my]):
                    ldg_w = E.inc(gp.dma_start(
                        out=w3_b[:, wi * 4096: wi * 4096 + ND * 256].rearrange(
                            "p (t c) -> p t c", t=ND),
                        in_=wsrc[:, g * 256:(g + 1) * 256].rearrange(
                            "(t p) c -> p t c", p=128)), "ldg", 16)
                qkv_vv = 0
                for r in range(4):
                    for sp in range(4):
                        bi = 0
                        E.wait("sync", "mm", xblk_mm[0])
                        ldx = E.inc(sy.dma_start(
                            out=xblk[bi][:, :].rearrange(
                                "p (t c) -> p t c", t=ND),
                            in_=agxn_out[r * 2048:(r + 1) * 2048,
                                         sp * 512:(sp + 1) * 512].rearrange(
                                "(t p) c -> p t c", p=128)), "ldw", 16)
                        E.wait("tensor", "ldw", ldx)
                        if r == 0 and sp == 0:
                            E.wait("tensor", "ldg", ldg_w)
                            if g == 0:
                                E.wait("tensor", "ldw", c0_ldw)
                                E.wait("tensor", "ldg", c0_ldg)
                        qg = 4 * r + sp
                        for hh in range(2):
                            last = None
                            for t in range(ND):
                                last = te.matmul(
                                    (psA if hh == 0 else psB)[:, :],
                                    w3_b[:, t * 256 + hh * 128:
                                         t * 256 + hh * 128 + 128],
                                    xblk[bi][:, t * 512:(t + 1) * 512],
                                    start=(t == 0), stop=(t == ND - 1))
                            E.inc(last, "mm")
                            last = None
                            for t in range(ND):
                                last = te.matmul(
                                    (psC if hh == 0 else psD)[:, :],
                                    w3_b[:, 4096 + t * 256 + hh * 128:
                                         4096 + t * 256 + hh * 128 + 128],
                                    xblk[bi][:, t * 512:(t + 1) * 512],
                                    start=(t == 0), stop=(t == ND - 1))
                            E.inc(last, "mm")
                        mm_qk = E.cnt["mm"]
                        for sub in range(4):
                            pv = psE if sub % 2 == 0 else psF
                            last = None
                            for t in range(ND):
                                last = te.matmul(
                                    pv[:, 0:256],
                                    xblk[bi][:, t * 512 + sub * 128:
                                             t * 512 + sub * 128 + 128],
                                    w3_b[:, 2 * 4096 + t * 256:
                                         2 * 4096 + (t + 1) * 256],
                                    start=(t == 0), stop=False)
                            last = te.matmul(
                                pv[:, 0:256], or_bf,
                                bv_sb[:, g * 256:(g + 1) * 256],
                                start=False, stop=True)
                            mmv = E.inc(last, "mm")
                            xblk_mm[0] = mmv
                            E.wait("vector", "mm", mmv)
                            kt = 16 * r + 4 * sp + sub
                            vv = E.inc(ve.tensor_copy(
                                v_b[:, kt * 256:(kt + 1) * 256], pv[:, 0:256]),
                                "vv")
                            E.wait("tensor", "vv", vv)
                        E.wait("vector", "mm", mm_qk)
                        if g == 0 and r == 0 and sp == 0:
                            E.wait("vector", "st")
                        for hh in range(2):
                            E.inc(ve.tensor_scalar(
                                qT_b[:, hh * 8192 + qg * 512:
                                     hh * 8192 + (qg + 1) * 512],
                                (psA if hh == 0 else psB)[:, :],
                                bq_sb[:, g * 2 + hh:g * 2 + hh + 1], None,
                                op0=ALU.add), "vv")
                            qkv_vv = E.inc(ve.tensor_scalar(
                                kT_b[:, hh * 8192 + qg * 512:
                                     hh * 8192 + (qg + 1) * 512],
                                (psC if hh == 0 else psD)[:, :],
                                bk_sb[:, g * 2 + hh:g * 2 + hh + 1], None,
                                op0=ALU.add), "vv")
                        E.wait("tensor", "vv", qkv_vv)
                for hh in range(2):
                    for qg in range(NQT_B):
                        nk = 4 * qg + 4
                        npair = nk // 2

                        def emit_pv(p, b, vvm, npair=npair, qg=qg, hh=hh):
                            E.wait("tensor", "vv", vvm)
                            last = None
                            for h2 in range(2):
                                ki = 2 * p + h2
                                te.matmul(psE[:, :],
                                          v_b[:, ki * 256 + hh * 128:
                                              ki * 256 + hh * 128 + 128],
                                          pT[b][:, h2 * 512:(h2 + 1) * 512],
                                          start=(ki == 0),
                                          stop=(ki == nk - 1))
                                last = te.matmul(
                                    psL[:, :], oc_bf,
                                    pT[b][:, h2 * 512:(h2 + 1) * 512],
                                    start=(ki == 0), stop=(ki == nk - 1))
                            pt_mm[b] = E.inc(last, "mm")

                        pend = None
                        for p in range(npair):
                            b = p % 2
                            sps = psG if b == 0 else psH
                            E.wait("tensor", "ex", sps_ex[b])
                            if p == 0:
                                E.wait("tensor", "vv", norm_vv[0])
                            last = None
                            for h2 in range(2):
                                ki = 2 * p + h2
                                last = te.matmul(
                                    sps[:, h2 * 512:(h2 + 1) * 512],
                                    kT_b[:, hh * 8192 + ki * 128:
                                         hh * 8192 + (ki + 1) * 128],
                                    qT_b[:, hh * 8192 + qg * 512:
                                         hh * 8192 + (qg + 1) * 512],
                                    start=True, stop=True)
                            mmv = E.inc(last, "mm")
                            if pend is not None:
                                emit_pv(*pend)
                            E.wait("scalar", "mm", mmv)
                            E.wait("scalar", "vv", pt_vv[b])
                            E.wait("scalar", "mm", pt_mm[b])
                            exv = E.inc(sc.activation(pT[b][:, :], sps[:, :],
                                                      AF.Exp, scale=ISC), "ex")
                            sps_ex[b] = exv
                            E.wait("vector", "ex", exv)
                            ins = None
                            for h2 in range(2):
                                ki = 2 * p + h2
                                if ki >= 4 * qg:
                                    off = 384 - 128 * (ki - 4 * qg)
                                    ins = ve.tensor_mul(
                                        pT[b][:, h2 * 512:(h2 + 1) * 512],
                                        pT[b][:, h2 * 512:(h2 + 1) * 512],
                                        mask_sb[:, off:off + 512])
                            if ins is None:
                                ins = ve.engine_nop()
                            vvm = E.inc(ins, "vv")
                            pt_vv[b] = vvm
                            pend = (p, b, vvm)
                        emit_pv(*pend)
                        E.wait("vector", "mm")
                        E.edge("vector", ve.reciprocal(l_inv, psL[:, :]))
                        E.wait("tensor", "vv")
                        mmb = E.inc(te.matmul(psC[:, :], or_f, l_inv,
                                              start=True, stop=True), "mm")
                        E.wait("vector", "mm", mmb)
                        E.edge("vector", ve.tensor_copy(l_bc[:, :], psC[:, :]))
                        par = qg % 2
                        if st_gs[par]:
                            E.wait("vector", st_gs_sem[par], st_gs[par])
                        vvn = E.inc(ve.tensor_mul(g_s[par][:, :], psE[:, :],
                                                  l_bc[:, :]), "vv")
                        norm_vv[0] = vvn
                        E.wait("sync", "vv", vvn)
                        st_gs[par] = E.inc(sy.dma_start(
                            out=ctxg_in[g][hh * 128:(hh + 1) * 128,
                                           qg * 512:(qg + 1) * 512],
                            in_=g_s[par][:, :]), "sta" if g == 0 else "stb",
                            16)
                        st_gs_sem[par] = "sta" if g == 0 else "stb"
                E.wait("gpsimd", "sta" if g == 0 else "stb")
                E.inc(gp.collective_compute(
                    "AllGather", ALU.bypass, replica_groups=G4,
                    ins=[ctxg_in[g].opt()], outs=[ctxg_out[g].opt()]), "cc")
            cc_a2a = E.cnt["cc"]

            wp_b = pb_bf(0, 32768)
            cx_b = pb_bf(65536, 16384)
            bc_a = pb_f(98304, 2048)
            E.wait("gpsimd", "mm")
            E.wait("sync", "mm")
            E.wait("sync", "cc", cc_a2a)
            E.wait("tensor", "ex")
            ldwp = 0
            for t in range(ND):
                ldwp = E.inc(gp.dma_start(
                    out=wp_b[:, t * 2048:(t + 1) * 2048],
                    in_=wp_full[t * 128:(t + 1) * 128, :]), "ldg", 16)
            lv = E.inc(sy.dma_start(out=row_a, in_=bp_row[:, 0:512]), "ldq", 16)
            lv = E.inc(sy.dma_start(out=row_b, in_=bp_row[:, 512:1024]), "ldq", 16)
            E.wait("tensor", "ldq", lv)
            bcast_row(row_a, bc_a[:, 0:512], 1, psA, psB)
            bcast_row(row_b, bc_a[:, 512:1024], 1, psA, psB)
            E.wait("sync", "vv")
            lv = E.inc(sy.dma_start(out=row_a, in_=bp_row[:, 1024:1536]), "ldq", 16)
            lv = E.inc(sy.dma_start(out=row_b, in_=bp_row[:, 1536:2048]), "ldq", 16)
            E.wait("tensor", "ldq", lv)
            bcast_row(row_a, bc_a[:, 1024:1536], 1, psA, psB)
            bcast_row(row_b, bc_a[:, 1536:2048], 1, psA, psB)
            if me == "sync":
                qq_off = (sy.partition_id() % 4) * T
            for half in range(2):
                ldc = 0
                for gH in range(2):
                    for hl in range(2):
                        if me == "sync":
                            src_ap = ctxg_out[gH].rearrange(
                                "(r p) c -> r p c", p=128)[hl::2][
                                :, :, bass.ds(qq_off + half * 1024, 1024)
                            ].rearrange("r p c -> p r c")
                        else:
                            src_ap = None
                        dst_ap = cx_b[:, :].rearrange(
                            "p (r g h c) -> p r g h c", r=4, g=2, h=2)[
                            :, :, gH, hl, :]
                        ldc = E.inc(sy.dma_start(out=dst_ap, in_=src_ap),
                                    "ld1", 16)
                E.wait("tensor", "ld1", ldc)
                if half == 0:
                    E.wait("tensor", "ldg", ldwp)
                for tt2 in range(8):
                    tt = half * 8 + tt2
                    lv = E.inc(sy.dma_start(
                        out=xf_a[:, :], in_=x_my[tt * 128:(tt + 1) * 128, :]),
                        "ldr", 16)
                    for j in range(4):
                        ps = psA if (tt2 * 4 + j) % 2 == 0 else psB
                        last = None
                        for Hh in range(16):
                            last = te.matmul(
                                ps[:, :],
                                cx_b[:, Hh * 1024 + tt2 * 128:
                                     Hh * 1024 + tt2 * 128 + 128],
                                wp_b[:, Hh * 2048 + j * 512:
                                     Hh * 2048 + (j + 1) * 512],
                                start=(Hh == 0), stop=(Hh == 15))
                        mmv = E.inc(last, "mm")
                        E.wait("vector", "mm", mmv)
                        if j == 0:
                            E.wait("vector", "ldr", lv)
                        E.edge("vector", ve.tensor_add(
                            xf_c[:, j * 512:(j + 1) * 512], ps[:, :],
                            bc_a[:, j * 512:(j + 1) * 512]))
                        vv = E.inc(ve.tensor_add(
                            xf_c[:, j * 512:(j + 1) * 512],
                            xf_c[:, j * 512:(j + 1) * 512],
                            xf_a[:, j * 512:(j + 1) * 512]), "vv")
                        E.wait("tensor", "vv", vv)
                    E.wait("sync", "vv")
                    stv = E.inc(sy.dma_start(out=x1_scr[tt], in_=xf_c[:, :]),
                                "st", 16)
                    E.wait("vector", "st", stv)
                    E.wait("sync", "vv")

            def y_ap(gi, c0, n):
                return vb(PB + gi * 1024, 512)[:, c0:c0 + n]

            def ffn(src_scr, dst_is_out, dst_scr, cc_wait):
                xnT_f = pb_bf(65536, 8192)
                slots = [pb_bf(81920 + k * 8192, 4096) for k in range(4)]
                dnA = [pb_bf(81920, 8192), pb_bf(98304, 8192)]
                dnB = pb_bf(114688, 8192)
                dres = pb_f(131072, 1024)
                E.wait("tensor", "cc", cc_wait)
                E.wait("sync", "cc", cc_wait)
                E.wait("sync", "st")
                for (tok0, ntok) in FFN_SPLITS:
                    ntt = ntok // 128
                    ag = {}
                    fln = {}
                    for tl in range(ntt):
                        tt = (tok0 // 128) + tl
                        xft = xf_a if tl % 2 == 0 else xf_e
                        xsem = "ldr" if tl % 2 == 0 else "lds"
                        if tl >= 2:
                            E.wait("sync", "vv", fln.get(tl - 2, 0))
                        lv = E.inc(sy.dma_start(out=xft[:, :],
                                                in_=src_scr[tt]), xsem, 16)
                        E.wait("vector", xsem, lv)
                        xnb = lnx[tl % 2]
                        vr = ln_tile(xft[:, :], xnb[:, :],
                                     pre_apply_wait=ag.get(tl - 2), bi=tl % 2)
                        fln[tl] = vr
                        lm, lvv = transpose16(xnb, tl * 128, 512, vr, xnT_f)
                        ag[tl] = lm
                        ln_done[0] = lvv
                    ld_slot = {}
                    mm_slot = {}
                    for ig in range(32):
                        sl_u = ig % 2
                        sl_g = 2 + ig % 2
                        i0 = ig * 2
                        r0 = i0 // 8
                        li = i0 % 8
                        if ig >= 2:
                            E.wait("sync", "mm", mm_slot.get(ig - 2, 0))
                        else:
                            E.wait("sync", "mm", dn_last[0])
                        lsem = "ld1" if ig % 2 == 0 else "ld2"
                        E.inc(sy.dma_start(
                            out=slots[sl_u][:, :].rearrange(
                                "p (t c) -> p t c", t=ND),
                            in_=fup_out[r0 * 2048:(r0 + 1) * 2048,
                                        li * 128:(li + 2) * 128].rearrange(
                                "(t p) c -> p t c", p=128)), lsem, 16)
                        ldv = E.inc(sy.dma_start(
                            out=slots[sl_g][:, :].rearrange(
                                "p (t c) -> p t c", t=ND),
                            in_=fga_out[r0 * 2048:(r0 + 1) * 2048,
                                        li * 128:(li + 2) * 128].rearrange(
                                "(t p) c -> p t c", p=128)), lsem, 16)
                        E.wait("tensor", lsem, ldv)
                        if ig == 0:
                            E.wait("tensor", "cc", cc_wait)
                        for i2 in range(2):
                            gi = i0 + i2
                            for s2 in range(1):
                                b2 = gi % 2
                                pu = psA if b2 == 0 else psB
                                pg = psC if b2 == 0 else psD
                                if ig == 0 and i2 == 0:
                                    E.wait("tensor", "vv", ln_done[0])
                                E.wait("tensor", "vv", upps.get(b2, 0))
                                E.wait("tensor", "ex", silups.get(b2, 0))
                                last = None
                                for t in range(ND):
                                    last = te.matmul(
                                        pu[:, :],
                                        slots[sl_u][:, t * 256 + i2 * 128:
                                                    t * 256 + i2 * 128 + 128],
                                        xnT_f[:, t * 512:(t + 1) * 512],
                                        start=(t == 0), stop=(t == ND - 1))
                                E.inc(last, "mm")
                                last = None
                                for t in range(ND):
                                    last = te.matmul(
                                        pg[:, :],
                                        slots[sl_g][:, t * 256 + i2 * 128:
                                                    t * 256 + i2 * 128 + 128],
                                        xnT_f[:, t * 512:(t + 1) * 512],
                                        start=(t == 0), stop=(t == ND - 1))
                                mmg = E.inc(last, "mm")
                                mm_slot[ig] = mmg
                                par = gi % 2
                                E.wait("scalar", "mm", mmg)
                                E.wait("scalar", "vv", upps.get(par, 0))
                                exv = E.inc(sc.activation(
                                    g_f[par][:, :], pg[:, :], AF.Silu), "ex")
                                silups[par] = exv
                                E.wait("vector", "ex", exv)
                                upps[par] = E.inc(ve.tensor_mul(
                                    y_ap(gi, 0, 512),
                                    g_f[par][:, :], pu[:, :]), "vv")
                    E.wait("tensor", "vv")
                    E.wait("sync", "mm")
                    for jp in range(8):
                        slA = dnA[jp % 2]
                        if jp >= 2:
                            E.wait("sync", "mm", dn_mm.get(jp - 2, 0) if True
                                   else 0)
                        dsem = "ld1" if jp % 2 == 0 else "ld2"
                        ldA = E.inc(sy.dma_start(
                            out=slA[:, :].rearrange("p (t c) -> p t c", t=32),
                            in_=fdn_out[jp * 8192: jp * 8192 + 4096,
                                        :].rearrange("(t p) c -> p t c",
                                                     p=128)), dsem, 16)
                        if jp >= 1:
                            E.wait("sync", "mm", dn_mmB.get(jp - 1, 0))
                        ldB = E.inc(sy.dma_start(
                            out=dnB[:, :].rearrange("p (t c) -> p t c", t=32),
                            in_=fdn_out[jp * 8192 + 4096: jp * 8192 + 8192,
                                        :].rearrange("(t p) c -> p t c",
                                                     p=128)), "ldq", 16)
                        oc0 = jp * 256
                        t0 = tok0 // 128
                        E.wait("sync", "vv", dnps.get(0, 0))
                        E.wait("sync", "vv", dnps.get(1, 0))
                        lv = E.inc(sy.dma_start(
                            out=dres[:, 0:ntt * 256].rearrange(
                                "p (t c) -> p t c", t=ntt),
                            in_=src_scr[t0:t0 + ntt, :,
                                        oc0:oc0 + 256].rearrange(
                                "t p c -> p t c")), "ld3", 16)
                        evv = 0
                        for tl in range(ntt):
                            tt = t0 + tl
                            pd = psE if tl % 2 == 0 else psF
                            E.wait("tensor", dsem, ldA if tl == 0 else 0)
                            E.wait("tensor", "vv", dnps.get(tl % 2, 0))
                            last = None
                            for t in range(32):
                                last = te.matmul(
                                    pd[:, 0:256],
                                    vb(PB + t * 1024, 512)[
                                        :, tl * 128: tl * 128 + 128],
                                    slA[:, t * 256:(t + 1) * 256],
                                    start=(t == 0), stop=False)
                            mmA = E.inc(last, "mm")
                            dn_mm[jp] = mmA
                            E.wait("tensor", "ldq", ldB if tl == 0 else 0)
                            last = None
                            for t in range(32, 64):
                                last = te.matmul(
                                    pd[:, 0:256],
                                    vb(PB + t * 1024, 512)[
                                        :, tl * 128: tl * 128 + 128],
                                    dnB[:, (t - 32) * 256:(t - 31) * 256],
                                    start=False, stop=(t == 63))
                            mmB = E.inc(last, "mm")
                            dn_mmB[jp] = mmB
                            dn_last[0] = mmB
                            E.wait("vector", "mm", mmB)
                            if tl == 0:
                                E.wait("vector", "ld3", lv)
                                E.wait("vector", "st", dn_st.get(jp - 1, 0))
                            evv = E.inc(ve.tensor_add(
                                xf_c[:, tl * 256:(tl + 1) * 256],
                                pd[:, 0:256],
                                dres[:, tl * 256:(tl + 1) * 256]), "vv")
                            dnps[tl % 2] = evv
                        E.wait("sync", "vv", evv)
                        if dst_is_out:
                            dst_ap = y_my[t0 * 128:(t0 + ntt) * 128,
                                          oc0:oc0 + 256].rearrange(
                                "(t p) c -> p t c", p=128)
                        else:
                            dst_ap = dst_scr[t0:t0 + ntt, :,
                                             oc0:oc0 + 256].rearrange(
                                "t p c -> p t c")
                        dn_st[jp] = E.inc(sy.dma_start(
                            out=dst_ap,
                            in_=xf_c[:, 0:ntt * 256].rearrange(
                                "p (t c) -> p t c", t=ntt)), "st", 16)

            dn_mm = {}
            dn_mmB = {}
            dnps = {}
            dn_st = {}
            upps = {}
            silups = {}
            dn_last = [0]
            ln_done = [0]
            ffn(x1_scr, False, x2_scr, cc_fold)

            xn3T = pb_bf(0, 32768)
            wKs = [pb_bf(65536, 8192), pb_bf(81920, 8192)]
            wVs = [pb_bf(98304, 8192), pb_bf(114688, 8192)]
            bc_a = pb_f(131072, 2048)
            tab = [pb_f(139264, 1024), pb_f(143360, 1024)]
            E.wait("sync", "st")
            E.wait("vector", "mm")
            E.wait("gpsimd", "mm")
            ag = {}
            xn3_st = {}
            abln = {}
            for tt in range(NT):
                if tt >= 1:
                    E.wait("sync", "vv", abln.get(tt - 1, 0))
                lv = E.inc(sy.dma_start(out=xf_a[:, :], in_=x2_scr[tt]),
                           "ldr", 16)
                E.wait("vector", "ldr", lv)
                xnb = lnx[tt % 2]
                if tt >= 2:
                    E.wait("vector", "st", xn3_st.get(tt - 2, 0))
                vr = ln_tile(xf_a[:, :], xnb[:, :],
                             pre_apply_wait=ag.get(tt - 2), bi=tt % 2)
                vr2 = E.inc(ve.tensor_scalar(
                    xf_e[:, :], xf_a[:, :], rstd[tt % 2][:, 0:1],
                    nb[tt % 2][:, 0:1], op0=ALU.mult, op1=ALU.add), "vv")
                abln[tt] = vr2
                E.wait("sync", "vv", vr2)
                xn3_st[tt] = E.inc(sy.dma_start(out=xn3_scr[tt],
                                                in_=xf_e[:, :]), "st", 16)
                E.wait("vector", "st", xn3_st[tt])
                lm, lvv = transpose16(xnb, tt * 128, 2048, vr, xn3T)
                ag[tt] = lm
            kv_mm = {}
            tab_vv = {}
            for cb in range(4):
                sl = cb % 2
                if cb >= 2:
                    E.wait("gpsimd", "mm", kv_mm.get(cb - 2, 0))
                last = E.inc(gp.dma_start(
                    out=wKs[sl][:, :].rearrange("p (t c) -> p t c", t=ND),
                    in_=w_kv[:, cb * 512:(cb + 1) * 512].rearrange(
                        "(t p) c -> p t c", p=128)), "ldg", 16)
                ldkv = E.inc(gp.dma_start(
                    out=wVs[sl][:, :].rearrange("p (t c) -> p t c", t=ND),
                    in_=w_kv[:, 2048 + cb * 512: 2048 + (cb + 1) * 512
                             ].rearrange("(t p) c -> p t c", p=128)), "ldg", 16)
                lv = E.inc(sy.dma_start(out=row_a,
                                        in_=bkv_row[:, cb * 512:(cb + 1) * 512]),
                           "ldq", 16)
                lv = E.inc(sy.dma_start(
                    out=row_b, in_=bkv_row[:, 2048 + cb * 512:
                                           2048 + (cb + 1) * 512]), "ldq", 16)
                E.wait("tensor", "ldq", lv)
                bcast_row(row_a, kvbK, 1, psC, psD)
                bcast_row(row_b, kvbV, 1, psC, psD)
                E.wait("sync", "vv")
                E.wait("tensor", "ldg", ldkv)
                for tt in range(NT):
                    last = None
                    for t in range(ND):
                        last = te.matmul(
                            psA[:, :], xn3T[:, t * 2048 + tt * 128:
                                            t * 2048 + tt * 128 + 128],
                            wKs[sl][:, t * 512:(t + 1) * 512],
                            start=(t == 0), stop=(t == ND - 1))
                    mmK = E.inc(last, "mm")
                    last = None
                    for t in range(ND):
                        last = te.matmul(
                            psB[:, :], xn3T[:, t * 2048 + tt * 128:
                                            t * 2048 + tt * 128 + 128],
                            wVs[sl][:, t * 512:(t + 1) * 512],
                            start=(t == 0), stop=(t == ND - 1))
                    mmV = E.inc(last, "mm")
                    E.wait("vector", "mm", mmV)
                    E.edge("vector", ve.tensor_add(kvK[:, :], psA[:, :],
                                                   kvbK[:, :]))
                    E.edge("vector", ve.tensor_add(kvV[:, :], psB[:, :],
                                                   kvbV[:, :]))
                    E.wait("tensor", "vv")
                    if cb < 2:
                        tb = tab[tt % 2]
                        tsem = "ld1" if tt % 2 == 0 else "ld2"
                        E.wait("sync", "vv", tab_vv.get((cb, tt - 2), 0))
                        lv = E.inc(sy.dma_start(
                            out=tb[:, 0:512],
                            in_=cos_my[tt * 128:(tt + 1) * 128,
                                       cb * 512:(cb + 1) * 512]), tsem, 16)
                        lv = E.inc(sy.dma_start(
                            out=tb[:, 512:1024],
                            in_=sin_my[tt * 128:(tt + 1) * 128,
                                       cb * 512:(cb + 1) * 512]), tsem, 16)
                        E.wait("vector", tsem, lv)
                        cK = [0]
                        cV = [0]
                        def vop(cnt, ins):
                            cnt[0] = E.inc(ins, "vv")
                        def vwait(cnt):
                            E.wait("vector", "vv", cnt[0])
                        vop(cK, ve.tensor_mul(kvt[:, 0:512], kvK, tb[:, 0:512]))
                        vop(cV, ve.tensor_mul(kvt[:, 512:1024], kvV,
                                              tb[:, 0:512]))
                        vop(cK, ve.scalar_tensor_tensor(
                            xf_c[:, 0:256], kvK[:, 1:512:2], -1.0,
                            tb[:, 512:1024:2], op0=ALU.mult, op1=ALU.mult))
                        vop(cV, ve.scalar_tensor_tensor(
                            xf_c[:, 256:512], kvV[:, 1:512:2], -1.0,
                            tb[:, 512:1024:2], op0=ALU.mult, op1=ALU.mult))
                        vwait(cK)
                        vop(cK, ve.tensor_add(kvt[:, 0:512:2],
                                              kvt[:, 0:512:2], xf_c[:, 0:256]))
                        vwait(cV)
                        vop(cV, ve.tensor_add(kvt[:, 512:1024:2],
                                              kvt[:, 512:1024:2],
                                              xf_c[:, 256:512]))
                        vop(cK, ve.scalar_tensor_tensor(
                            xf_c[:, 0:256], kvK[:, 0:512:2], 1.0,
                            tb[:, 513:1024:2], op0=ALU.mult, op1=ALU.mult))
                        vop(cV, ve.scalar_tensor_tensor(
                            xf_c[:, 256:512], kvV[:, 0:512:2], 1.0,
                            tb[:, 513:1024:2], op0=ALU.mult, op1=ALU.mult))
                        vwait(cK)
                        vop(cK, ve.tensor_add(kvt[:, 1:512:2],
                                              kvt[:, 1:512:2], xf_c[:, 0:256]))
                        vwait(cV)
                        vop(cV, ve.tensor_add(kvt[:, 513:1024:2],
                                              kvt[:, 513:1024:2],
                                              xf_c[:, 256:512]))
                        vwait(cK)
                        vwait(cV)
                        vv = E.inc(ve.tensor_copy(kv2[:, :], kvt[:, :]), "vv")
                        E.wait("scalar", "vv", vv)
                        tab_vv[(cb, tt)] = E.cnt["vv"]
                    else:
                        E.wait("scalar", "vv")
                    E.edge("scalar", sc.activation(kv2[:, :], kv2[:, :],
                                                   AF.Abs))
                    E.edge("scalar", sc.activation(
                        kv2[:, :], kv2[:, :], AF.Sin, scale=-0.125,
                        bias=scon[:, 0:1]))
                    E.edge("scalar", sc.activation(kv2[:, :], kv2[:, :],
                                                   AF.Square, scale=SQ2))
                    E.edge("scalar", sc.activation(
                        kv2[:, :], kv2[:, :], AF.Square, scale=SQ2,
                        bias=scon[:, 1:2]))
                    E.edge("scalar", sc.activation(
                        kv2[:, :], kv2[:, :], AF.Square, scale=SQ2,
                        bias=scon[:, 1:2]))
                    exv = E.inc(sc.activation(kv2[:, :], kv2[:, :],
                                              AF.Copy, bias=-1.0), "ex")
                    E.wait("vector", "ex", exv)
                    vv = E.inc(ve.tensor_mul(kvt[:, 0:512], kvK, kvV), "vv")
                    E.wait("tensor", "vv", vv)
                    kv_mm[cb] = E.inc(te.matmul(
                        psL[:, :], oc_f, kvt[:, 0:512],
                        start=(tt == 0), stop=(tt == NT - 1)), "mm")
                E.wait("vector", "mm")
                vv = E.inc(ve.tensor_copy(row_a, psL[:, :]), "vv")
                E.wait("sync", "vv", vv)
                stv = E.inc(sy.dma_start(out=ar_in[:, cb * 512:(cb + 1) * 512],
                                         in_=row_a), "st", 16)
                E.wait("tensor", "st", stv)
                E.wait("vector", "st", stv)
            E.wait("gpsimd", "st")
            cc_ar = E.inc(gp.collective_compute(
                "AllReduce", ALU.add, replica_groups=G4,
                ins=[ar_in.opt()], outs=[ar_out.opt()]), "cc")
            E.wait("sync", "cc", cc_ar)
            E.wait("tensor", "cc", cc_ar)
            for s5 in range(4):
                lv = E.inc(sy.dma_start(
                    out=row_a, in_=ar_out[:, s5 * 512:(s5 + 1) * 512]),
                    "ldq", 16)
                E.wait("tensor", "ldq", lv)
                bcast_row(row_a, bc_a[:, s5 * 512:(s5 + 1) * 512], 1, psC, psD,
                          extra=(0.5, 0.5))
                E.wait("sync", "vv")
            for tt in range(NT):
                lv = E.inc(sy.dma_start(out=xf_a[:, :], in_=x2_scr[tt]),
                           "ldw", 16)
                lv2 = E.inc(sy.dma_start(out=xf_e[:, :],
                                         in_=xn3_scr[tt]), "ldw", 16)
                E.wait("vector", "ldw", lv2)
                E.edge("vector", ve.tensor_mul(xf_c[:, :], xf_e[:, :],
                                               bc_a[:, :]))
                vv = E.inc(ve.tensor_add(xf_c[:, :], xf_c[:, :], xf_a[:, :]),
                           "vv")
                E.wait("sync", "vv", vv)
                stv = E.inc(sy.dma_start(out=x3_scr[tt], in_=xf_c[:, :]),
                            "st", 16)
                E.wait("vector", "st", stv)
                E.wait("sync", "vv", vv)

            ffn(x3_scr, True, None, cc_fold)

            if taps:
                E.wait("sync", "st")
                for nm, srcd in [("t_x1", x1_scr), ("t_x2", x2_scr),
                                 ("t_x3", x3_scr), ("t_xn3", xn3_scr)]:
                    E.inc(sy.dma_start(
                        out=tap[nm],
                        in_=srcd.rearrange("t p d -> (t p) d")), "st", 16)
                E.inc(sy.dma_start(out=tap["t_agxn"], in_=agxn_out), "st", 16)
                E.inc(sy.dma_start(out=tap["t_fup"], in_=fup_out), "st", 16)
                E.inc(sy.dma_start(out=tap["t_fdn"], in_=fdn_out), "st", 16)
                E.inc(sy.dma_start(out=tap["t_ctx0"], in_=ctxg_out[0]), "st", 16)
                E.inc(sy.dma_start(out=tap["t_ctx1"], in_=ctxg_out[1]), "st", 16)
                E.inc(sy.dma_start(out=tap["t_fin"], in_=fup_in), "st", 16)
                E.inc(sy.dma_start(out=tap["t_ar"], in_=ar_out), "st", 16)
            E.wait("sync", "st")
            E.wait("gpsimd", "st")

        @block.sync
        def _(e):
            emit("sync", e)

        @block.tensor
        def _(e):
            emit("tensor", e)

        @block.scalar
        def _(e):
            emit("scalar", e)

        @block.vector
        def _(e):
            emit("vector", e)

        @block.gpsimd
        def _(e):
            emit("gpsimd", e)

    return nc


def _prep_inputs(ins):
    x = np.asarray(ins["L0_attn_a_x"], dtype=np.float32)
    wq = np.asarray(ins["L0_attn_a_wq"], np.float32)
    bq = np.asarray(ins["L0_attn_a_bq"], np.float32)
    wk = np.asarray(ins["L0_attn_a_wk"], np.float32)
    bk = np.asarray(ins["L0_attn_a_bk"], np.float32)
    wv = np.asarray(ins["L0_attn_a_wv"], np.float32)
    bv = np.asarray(ins["L0_attn_a_bv"], np.float32)
    wp = np.ascontiguousarray(np.asarray(ins["L0_attn_a_wp"], np.float32))
    bp = np.asarray(ins["L0_attn_a_bp"], np.float32)
    wkv = np.ascontiguousarray(np.asarray(ins["L0_attn_b_W"], np.float32))
    bkv = np.asarray(ins["L0_attn_b_b"], np.float32)
    su_ug = np.asarray(ins["L0_ffn_SU_ug"], np.float32)
    su_dp = np.asarray(ins["L0_ffn_SU_dp"], np.float32)
    wsc_ug = np.asarray(ins["L0_ffn_Wscale_ug"], np.float32)
    wsc_dp = np.asarray(ins["L0_ffn_Wscale_dp"], np.float32)
    w_up = np.asarray(ins["L0_ffn_W_up"], np.float32)
    w_gate = np.asarray(ins["L0_ffn_W_gate"], np.float32)
    w_down = np.asarray(ins["L0_ffn_W_down"], np.float32)

    rot = 1024
    inv_freq = 1.0 / (10000.0 ** (np.arange(0, rot, 2, dtype=np.float64) / rot))
    su_ug_t = np.ascontiguousarray(su_ug.reshape(ND, 128).T)
    su_dp_t = np.ascontiguousarray(su_dp.reshape(NI, 128).T)

    in_maps = []
    for c in range(8):
        b = c // 4
        qq = c % 4
        h0 = 4 * qq
        pos = np.arange(qq * T, (qq + 1) * T, dtype=np.float64)
        fr = pos[:, None] * inv_freq[None, :]
        cos_e = np.repeat(np.cos(fr), 2, axis=1).astype(np.float32)
        sin_e = np.repeat(np.sin(fr), 2, axis=1).astype(np.float32)
        m = {
            "x_my": np.ascontiguousarray(x[b, qq * T:(qq + 1) * T, :]),
            "wq_my": np.ascontiguousarray(wq[:, h0 * 128:(h0 + 4) * 128]),
            "wk_my": np.ascontiguousarray(wk[:, h0 * 128:(h0 + 4) * 128]),
            "wv_my": np.ascontiguousarray(wv[:, h0 * 128:(h0 + 4) * 128]),
            "bq_my": np.ascontiguousarray(
                bq[h0 * 128:(h0 + 4) * 128].reshape(4, 128).T),
            "bk_my": np.ascontiguousarray(
                bk[h0 * 128:(h0 + 4) * 128].reshape(4, 128).T),
            "bv_row": np.ascontiguousarray(
                bv[h0 * 128:(h0 + 4) * 128].reshape(1, 512)),
            "wp_full": wp,
            "bp_row": np.ascontiguousarray(bp.reshape(1, D)),
            "wup_my": np.ascontiguousarray(w_up[c * 1024:(c + 1) * 1024, :]),
            "wgate_my": np.ascontiguousarray(
                w_gate[c * 1024:(c + 1) * 1024, :]),
            "wdown_my": np.ascontiguousarray(w_down[c * 256:(c + 1) * 256, :]),
            "su_ug_t": su_ug_t,
            "su_dp_t": su_dp_t,
            "wsc_up": np.ascontiguousarray(
                wsc_ug[c * 1024:(c + 1) * 1024].reshape(1, 1024)),
            "wsc_gate": np.ascontiguousarray(
                wsc_ug[INTER + c * 1024: INTER + (c + 1) * 1024
                       ].reshape(1, 1024)),
            "wsc_dp": np.ascontiguousarray(
                wsc_dp[c * 256:(c + 1) * 256].reshape(1, 256)),
            "w_kv": wkv,
            "bkv_row": np.ascontiguousarray(bkv.reshape(1, KVC)),
            "cos_my": cos_e,
            "sin_my": sin_e,
        }
        in_maps.append(m)
    return in_maps


def kernel(**inputs):
    nc = build_program()
    in_maps = _prep_inputs(inputs)
    res = run_bass_kernel_spmd(nc, in_maps, core_ids=list(range(8)))
    out = np.empty((B, S, D), dtype=np.float32)
    for c in range(8):
        b = c // 4
        qq = c % 4
        out[b, qq * T:(qq + 1) * T, :] = res.results[c]["y_my"]
    return out
